# version 15
# baseline (speedup 1.0000x reference)
import numpy as np
import ml_dtypes
import concourse.bass as bass
import concourse.mybir as mybir
from concourse.bass_utils import run_bass_kernel_spmd

F32 = mybir.dt.float32
BF16 = mybir.dt.bfloat16
AX = mybir.AxisListType
ALU = mybir.AluOpType
ACT = mybir.ActivationFunctionType

D = 1024
H = 16
DH = 64
E = 8
SCALE = DH ** -0.5
EPS_ATTN = 1e-8
EPS_LN = 1e-5



class EngW:

    def __init__(self, eng):
        self._e = eng
        self.last = None

    def __getattr__(self, name):
        f = getattr(self._e, name)
        if not callable(f):
            return f
        rec = name not in ("wait_ge", "nop")

        def w(*a, **k):
            r = f(*a, **k)
            if rec and hasattr(r, "then_inc"):
                self.last = r
            return r
        return w

    def mark(self, sem, ct):
        self.last.then_inc(sem, ct.add(1))


class Ct:
    def __init__(self):
        self.v = 0

    def add(self, n):
        self.v += n
        return n


def build(T=4096, dbg=None):
    QB = 256
    NQB = T // QB
    NKC = T // 128

    nc = bass.Bass("TRN2", target_bir_lowering=False)

    x_d = nc.dram_tensor("x", [T, D], F32, kind="ExternalInput").ap()
    wq_d = nc.dram_tensor("wq", [D, D], F32, kind="ExternalInput").ap()
    wkv_d = nc.dram_tensor("wkv", [D, 2 * D], F32, kind="ExternalInput").ap()
    wo_d = nc.dram_tensor("wo", [D, D], F32, kind="ExternalInput").ap()
    bo_d = nc.dram_tensor("bo_b", [128, D], F32, kind="ExternalInput").ap()
    gwt_d = nc.dram_tensor("gwt", [D, E], F32, kind="ExternalInput").ap()
    wet_d = nc.dram_tensor("wet", [E, D, D], F32, kind="ExternalInput").ap()
    idbf_d = nc.dram_tensor("id_bf", [128, 128], BF16, kind="ExternalInput").ap()
    idf32_d = nc.dram_tensor("id_f32", [128, 128], F32, kind="ExternalInput").ap()
    tri_d = nc.dram_tensor("tri", [128, 512], BF16, kind="ExternalInput").ap()
    iota_d = nc.dram_tensor("iota8", [128, 16], F32, kind="ExternalInput").ap()
    ones_d = nc.dram_tensor("ones_f", [1, 128], F32, kind="ExternalInput").ap()
    cst_d = nc.dram_tensor("cst", [128, 8], F32, kind="ExternalInput").ap()
    out_d = nc.dram_tensor("out", [T, D], F32, kind="ExternalOutput").ap()
    x2_d = nc.dram_tensor("x2scr", [T, D], F32, kind="Internal").ap()

    sLD = nc.alloc_semaphore("sLD")
    sLX = nc.alloc_semaphore("sLX")
    sLC = nc.alloc_semaphore("sLC")
    sGD = nc.alloc_semaphore("sGD")
    sST = nc.alloc_semaphore("sST")
    sSH = nc.alloc_semaphore("sSH")
    sTE = nc.alloc_semaphore("sTE")
    sSC = nc.alloc_semaphore("sSC")
    sVE = nc.alloc_semaphore("sVE")
    sGP = nc.alloc_semaphore("sGP")
    cLD, cLX, cLC, cGD, cST, cSH, cTE, cSC, cVE, cGP = (Ct() for _ in range(10))

    ARENA_BYTES = 210800
    _arena = nc.alloc_sbuf_tensor("ARENA", [128, ARENA_BYTES], mybir.dt.uint8)
    _base = nc.lookup_mloc(_arena).addr
    _off = [_base]

    def _at(name, shape, dtype, region_end=None):
        nbytes = shape[1] * (2 if dtype == BF16 else 4)
        addr = (_off[0] + 31) & ~31
        h = nc.alloc_sbuf_tensor_at(name, shape, dtype, offset=addr)
        _off[0] = addr + nbytes
        return h

    XT = _at("XT", [128, 2 * D], F32)
    _x2s_off = (_off[0] + 31) & ~31
    X2S = _at("X2S", [128, 2048], F32)
    SCR = _at("SCR", [128, 128], F32)
    BOB = _at("BOB", [128, D], BF16)
    IDBF = _at("IDBF", [128, 128], BF16)
    IDF = _at("IDF", [128, 128], F32)
    TRI = _at("TRI", [128, 512], BF16)
    IOTA = _at("IOTA", [128, 16], F32)
    ONESF = _at("ONESF", [1, 128], F32)
    CST = _at("CST", [128, 8], F32)
    GWT = _at("GWT", [128, 64], F32)
    MQ = _at("MQ", [128, 4], F32)
    EQ8 = _at("EQ8", [128, 16], F32)
    IDX = _at("IDX", [128, 2], F32)
    TV = _at("TV", [128, 64], F32)
    PMAX = _at("PMAX", [128, 32], F32)
    MH = _at("MH", [128, 1024], BF16)
    CMAX = _at("CMAX", [128, 32], F32)
    LNTM = _at("LNTM", [128, D], BF16)
    LNTMB = _at("LNTMB", [128, D], BF16)
    ATTS = nc.alloc_sbuf_tensor_at("ATTS", [128, 2048], BF16, offset=_x2s_off)
    MLN = _at("MLN", [128, 512], BF16)
    shared_end = _off[0]
    _k_off = (_off[0] + 31) & ~31
    WB = _at("WB", [128, 8 * D], BF16)
    K_sb = _at("K", [128, 8 * T], BF16)
    V_sb = _at("V", [128, NKC * (H * 65)], BF16)
    QF = _at("QF", [128, 4096], BF16)
    PB = _at("PB", [128, 2048], BF16)
    PBM = _at("PBM", [128, 1536], BF16)
    ATTN = _at("ATTN", [128, 2048], BF16)
    LNFM = _at("LNFM", [128, 2048], BF16)
    NUMBUF = _at("NUMBUF", [128, 4096], BF16)
    a_end = _off[0]
    _off[0] = shared_end
    WT = _at("WT", [128, 64 * D], BF16)
    LN2TM = _at("LN2TM", [128, 2048], F32)
    LN2F32 = _at("LN2F32", [128, 2048], F32)
    MBC = _at("MBC", [128, 2048], BF16)
    XTB2 = _at("XTB2", [128, 2048], F32)
    LN2TMb = _at("LN2TMb", [128, 2048], F32)
    DUMB = _at("DUMB", [128, 2048], BF16)
    b_end = _off[0]
    assert max(a_end, b_end) < _base + ARENA_BYTES, (_base, shared_end, a_end, b_end)
    LN2F16 = LN2F32
    PSA = nc.alloc_psum_tensor("PSA", [128, 2048], F32)
    PSB = nc.alloc_psum_tensor("PSB", [128, 512], F32)
    PSB16 = nc.alloc_psum_tensor("PSB16", [128, 1024], BF16)
    PSN = nc.alloc_psum_tensor("PSN", [128, 512], F32)
    PSC = nc.alloc_psum_tensor("PSC", [128, 512], F32)

    sync, gp, te, sc, ve = (EngW(nc.sync), EngW(nc.gpsimd), EngW(nc.tensor),
                            EngW(nc.scalar), EngW(nc.vector))

    sync.dma_start(out=IDBF[:, :], in_=idbf_d).then_inc(sLC, cLC.add(16))
    sync.dma_start(out=IDF[:, :], in_=idf32_d).then_inc(sLC, cLC.add(16))
    sync.dma_start(out=TRI[:, :], in_=tri_d).then_inc(sLC, cLC.add(16))
    sync.dma_start(out=IOTA[:, :], in_=iota_d).then_inc(sLC, cLC.add(16))
    sync.dma_start(out=ONESF[:, :], in_=ones_d).then_inc(sLC, cLC.add(16))
    sync.dma_start(out=CST[:, :], in_=cst_d).then_inc(sLC, cLC.add(16))
    gp.dma_start(out=BOB[:, :], in_=bo_d).then_inc(sGD, cGD.add(16))
    sync.dma_start(out=GWT.ap().rearrange("p (hc e) -> p hc e", hc=8),
                   in_=gwt_d.rearrange("(hc p) e -> p hc e", p=128)).then_inc(sLC, cLC.add(16))
    gp.dma_start(out=WB.ap().rearrange("p (dc i) -> p dc i", dc=8),
                 in_=wkv_d.rearrange("(dc p) i -> p dc i", p=128)[:, :, 0:D]
                 ).then_inc(sGD, cGD.add(16))
    ve.wait_ge(sLC, cLC.v)
    sc.wait_ge(sLC, cLC.v)
    te.wait_ge(sLC, cLC.v)
    ve.memset(V_sb.ap().rearrange("p (k h j) -> p k h j", k=NKC, h=H)[:, :, :, 64:65], 1.0)
    ve.mark(sVE, cVE)

    x_t = x_d.rearrange("(n s p) c -> n p s c", s=2, p=128)
    x2_t = x2_d.rearrange("(n s p) c -> n p s c", s=2, p=128)
    out_t = out_d.rearrange("(n s p) c -> n p s c", s=2, p=128)
    xt_ap = XT.ap().rearrange("p (s c) -> p s c", s=2)
    x2s_ap = X2S.ap().rearrange("p (s c) -> p s c", s=2)

    def ln_pair(out0, out1, dum=None, srcb=None):
        cols = [SCR[:, c:c + 1] for c in range(16)]
        (s_sum0, s_ss0, s_m0, s_e0, s_v0, s_l0, s_r0, _,
         s_sum1, s_ss1, s_m1, s_e1, s_v1, s_l1, s_r1, _) = cols
        if srcb is None:
            srcb = XT
        src0 = srcb[:, 0:D]
        src1 = srcb[:, D:2 * D]
        ve.reduce_sum(s_sum0, src0, axis=AX.X)
        ve.reduce_sum(s_sum1, src1, axis=AX.X)
        ve.scalar_tensor_tensor(dum[:, 0:D], src0, 1.0, src0, ALU.mult, ALU.mult,
                                accum_out=s_ss0)
        ve.scalar_tensor_tensor(dum[:, D:2 * D], src1, 1.0, src1, ALU.mult, ALU.mult,
                                accum_out=s_ss1).then_inc(sVE, cVE.add(1))
        ve.wait_ge(sVE, cVE.v)
        ve.tensor_scalar_mul(s_m0, s_sum0, 1.0 / D)
        ve.tensor_scalar_mul(s_m1, s_sum1, 1.0 / D)
        ve.tensor_scalar_mul(s_e0, s_ss0, 1.0 / D)
        ve.tensor_scalar_mul(s_e1, s_ss1, 1.0 / D).then_inc(sVE, cVE.add(1))
        ve.wait_ge(sVE, cVE.v)
        ve.tensor_mul(s_v0, s_m0, s_m0)
        ve.tensor_mul(s_v1, s_m1, s_m1).then_inc(sVE, cVE.add(1))
        ve.wait_ge(sVE, cVE.v)
        ve.tensor_sub(s_v0, s_e0, s_v0)
        ve.tensor_sub(s_v1, s_e1, s_v1).then_inc(sVE, cVE.add(1))
        sc.wait_ge(sVE, cVE.v)
        sc.activation(s_l0, s_v0, ACT.Ln, bias=CST[:, 0:1])
        sc.activation(s_l1, s_v1, ACT.Ln, bias=CST[:, 0:1]).then_inc(sSC, cSC.add(1))
        sc.wait_ge(sSC, cSC.v)
        sc.activation(s_r0, s_l0, ACT.Exp, scale=CST[:, 1:2])
        sc.activation(s_r1, s_l1, ACT.Exp, scale=CST[:, 1:2]).then_inc(sSC, cSC.add(1))
        ve.wait_ge(sSC, cSC.v)
        ve.tensor_scalar(out0, src0, s_m0[:, 0:1], s_r0[:, 0:1], ALU.subtract, ALU.mult)
        ve.tensor_scalar(out1, src1, s_m1[:, 0:1], s_r1[:, 0:1],
                         ALU.subtract, ALU.mult).then_inc(sVE, cVE.add(1))


    def emit_ln_to_fm(dst_fm, f32=False, dst32=None, srcb=None):
        if f32:
            ln_pair(dst32[:, 0:D], dst32[:, D:2 * D], dum=DUMB, srcb=srcb)
            src0, src1 = dst32[:, 0:D], dst32[:, D:2 * D]
        else:
            ln_pair(LNTM[:, :], LNTMB[:, :], dum=PB, srcb=srcb)
            src0, src1 = LNTM[:, :], LNTMB[:, :]
        te.wait_ge(sVE, cVE.v)
        for sub, src in ((0, src0), (1, src1)):
            if f32:
                for rnd in range(2):
                    te.wait_ge(sSC, cSC.v)
                    for c in range(4):
                        dc = rnd * 4 + c
                        te.transpose(PSB[:, c * 128:(c + 1) * 128],
                                     src[:, dc * 128:(dc + 1) * 128], IDF[:, :])
                    te.mark(sTE, cTE)
                    sc.wait_ge(sTE, cTE.v)
                    dst = dst_fm.ap().rearrange("p (dc t) -> p dc t", dc=8)[
                        :, rnd * 4:(rnd + 1) * 4, sub * 128:(sub + 1) * 128]
                    sc.copy(dst, PSB.ap().rearrange("p (c t) -> p c t", c=4)
                            ).then_inc(sSC, cSC.add(1))
            else:
                te.wait_ge(sSC, cSC.v)
                for c in range(8):
                    te.transpose(PSB16[:, c * 128:(c + 1) * 128],
                                 src[:, c * 128:(c + 1) * 128], IDBF[:, :])
                te.mark(sTE, cTE)
                sc.wait_ge(sTE, cTE.v)
                dst = dst_fm.ap().rearrange("p (dc t) -> p dc t", dc=8)[
                    :, :, sub * 128:(sub + 1) * 128]
                sc.copy(dst, PSB16.ap().rearrange("p (dc t) -> p dc t", dc=8)
                        ).then_inc(sSC, cSC.add(1))
        te.wait_ge(sSC, cSC.v)

    def proj_1024(dst_sb, rhs_cols=256):
        for ic in range(8):
            bank = ic % 4
            if ic >= 4:
                te.wait_ge(sSC, cSC.v)
            for dc in range(8):
                te.matmul(PSA[:, bank * 512:bank * 512 + 256],
                          WB[:, dc * D + ic * 128:dc * D + (ic + 1) * 128],
                          LNFM[:, dc * 256:(dc + 1) * 256],
                          start=(dc == 0), stop=(dc == 7))
            te.mark(sTE, cTE)
            sc.wait_ge(sTE, cTE.v)
            sc.copy(dst_sb(ic), PSA[:, bank * 512:bank * 512 + 256]
                    ).then_inc(sSC, cSC.add(1))

    for pas in (0, 1):
        if pas == 1:
            gp.wait_ge(sTE, cTE.v)
            gp.dma_start(out=WB.ap().rearrange("p (dc i) -> p dc i", dc=8),
                         in_=wkv_d.rearrange("(dc p) i -> p dc i", p=128)[:, :, D:2 * D]
                         ).then_inc(sGD, cGD.add(16))
        for tt in range(NQB):
            sync.wait_ge(sVE, cVE.v)
            sync.dma_start(out=xt_ap, in_=x_t[tt]).then_inc(sLD, cLD.add(16))
            ve.wait_ge(sLD, cLD.v)
            if pas == 0 and tt == 0:
                ve.wait_ge(sGD, cGD.v)
            emit_ln_to_fm(LNFM)
            te.wait_ge(sGD, cGD.v)
            if pas == 0:
                proj_1024(lambda ic, _tt=tt: K_sb[:, ic * T + _tt * 256: ic * T + (_tt + 1) * 256])
            else:
                for tc in range(2):
                    half = tc % 2
                    te.wait_ge(sSC, cSC.v)
                    for dc in range(8):
                        lhs = LNFM[:, dc * 256 + tc * 128: dc * 256 + (tc + 1) * 128]
                        te.matmul(PSA[:, half * 1024:half * 1024 + 512],
                                  lhs, WB[:, dc * D:dc * D + 512],
                                  start=(dc == 0), stop=(dc == 7))
                        te.matmul(PSA[:, half * 1024 + 512:half * 1024 + 1024],
                                  lhs, WB[:, dc * D + 512:dc * D + 1024],
                                  start=(dc == 0), stop=(dc == 7))
                    te.mark(sTE, cTE)
                    kc = tt * 2 + tc
                    sc.wait_ge(sTE, cTE.v)
                    sc.copy(V_sb.ap().rearrange("p (k h j) -> p k h j", k=NKC, h=H)
                            [:, kc, :, 0:64],
                            PSA.ap().rearrange("p (g h j) -> p g h j", g=2, h=H)[:, half]
                            ).then_inc(sSC, cSC.add(1))
            ve.wait_ge(sSC, cSC.v)
            ve.memset(MQ[:, 3:4], 0.0)
            ve.mark(sVE, cVE)

    nc.all_engine_barrier()

    xt_free_ve = 0
    ln_ready_ve = 0
    m_ve = [0, 0]
    m_sc = [0, 0]
    odd_scr_ve = [0, 0]
    tvexp_box = [0]
    masked_done_sc = {}

    def masked_units_for(mq):
        qbase = 2048 * (mq % 2)
        tvo = 32 * (mq % 2)
        units = []

        def mk_piece(hp, qt, pz, k0):
            def u():
                ic = hp
                a = k0 + pz * 512
                b = min(a + 512, T)
                lhs_e = QF[0:64, qbase + ic * 256 + qt * 128:
                            qbase + ic * 256 + (qt + 1) * 128]
                lhs_o = QF[64:128, qbase + ic * 256 + qt * 128:
                           qbase + ic * 256 + (qt + 1) * 128]
                if m_ve[0]:
                    te.wait_ge(sVE, m_ve[0])
                if m_sc[0]:
                    te.wait_ge(sSC, m_sc[0])
                te.matmul(PSA[:, 1024: 1024 + (b - a)],
                          lhs_e, K_sb[0:64, ic * T + a: ic * T + b],
                          start=True, stop=True, tile_position=(0, 0))
                te.matmul(PSA[:, 1536: 1536 + (b - a)],
                          lhs_o, K_sb[64:128, ic * T + a: ic * T + b],
                          start=True, stop=True, tile_position=(64, 0))
                te.mark(sTE, cTE)
                ve.wait_ge(sTE, cTE.v)
                if pz == 0:
                    ve.wait_ge(sVE, cVE.v)
                ve.reduce_max(PMAX[:, qt * 16 + pz:qt * 16 + pz + 1],
                              PSA[:, 1024: 1024 + (b - a)],
                              axis=AX.X).then_inc(sVE, cVE.add(1))
                m_ve[0] = cVE.v
                sc.wait_ge(sTE, cTE.v)
                if odd_scr_ve[0]:
                    sc.wait_ge(sVE, odd_scr_ve[0])
                sc.activation(PBM[:, 0:(b - a)],
                              PSA[:, 1536: 1536 + (b - a)],
                              ACT.Exp, scale=CST[:, 2:3]).then_inc(sSC, cSC.add(1))
                m_sc[0] = cSC.v
                ve.wait_ge(sSC, cSC.v)
                ve.wait_ge(sVE, cVE.v)
                if pz == 0:
                    ve.tensor_copy(PBM[:, 1024:1024 + (b - a)],
                                   PBM[:, 0:(b - a)]).then_inc(sVE, cVE.add(1))
                else:
                    ve.tensor_max(PBM[:, 1024:1024 + (b - a)],
                                  PBM[:, 1024:1024 + (b - a)],
                                  PBM[:, 0:(b - a)]).then_inc(sVE, cVE.add(1))
                odd_scr_ve[0] = cVE.v
            return u

        def mk_final(hp, qt, npc, w0):
            def u():
                h_e, h_o = 2 * hp, 2 * hp + 1
                ve.wait_ge(sVE, cVE.v)
                if tvexp_box[0]:
                    ve.wait_ge(sSC, tvexp_box[0])
                ve.reduce_max(MQ[:, qt:qt + 1], PMAX[:, qt * 16:qt * 16 + npc],
                              axis=AX.X)
                ve.reduce_max(TV[:, tvo + h_o * 2 + qt: tvo + h_o * 2 + qt + 1],
                              PBM[:, 1024:1024 + w0], axis=AX.X)
                ve.mark(sVE, cVE)
                sc.wait_ge(sVE, cVE.v)
                sc.activation(TV[:, tvo + h_e * 2 + qt: tvo + h_e * 2 + qt + 1],
                              MQ[:, qt:qt + 1], ACT.Exp, scale=CST[:, 2:3]
                              ).then_inc(sSC, cSC.add(1))
                tvexp_box[0] = cSC.v
            return u

        for hp in range(8):
            for qt in range(2):
                k0 = 128 * (mq * 2 + qt)
                npc = (T - k0 + 511) // 512
                for pz in range(npc):
                    units.append(mk_piece(hp, qt, pz, k0))
                units.append(mk_final(hp, qt, npc, min(512, T - k0)))
        return units

    def emit_lnfm_from_tm():
        te.wait_ge(sVE, cVE.v)
        for sub, src in ((0, LNTM[:, :]), (1, LNTMB[:, :])):
            te.wait_ge(sSC, cSC.v)
            for c in range(8):
                te.transpose(PSB16[:, c * 128:(c + 1) * 128],
                             src[:, c * 128:(c + 1) * 128], IDBF[:, :])
            te.mark(sTE, cTE)
            sc.wait_ge(sTE, cTE.v)
            dst = LNFM.ap().rearrange("p (dc t) -> p dc t", dc=8)[
                :, :, sub * 128:(sub + 1) * 128]
            sc.copy(dst, PSB16.ap().rearrange("p (dc t) -> p dc t", dc=8)
                    ).then_inc(sSC, cSC.add(1))

    gp.wait_ge(sTE, cTE.v)
    gp.dma_start(out=WB.ap().rearrange("p (dc i) -> p dc i", dc=8),
                 in_=wq_d.rearrange("(dc p) i -> p dc i", p=128)).then_inc(sGD, cGD.add(16))
    wq_gd = cGD.v
    sync.wait_ge(sVE, cVE.v)
    sync.dma_start(out=xt_ap, in_=x_t[0]).then_inc(sLD, cLD.add(16))
    ve.wait_ge(sLD, cLD.v)
    emit_ln_to_fm(LNFM, srcb=XT)
    te.wait_ge(sGD, wq_gd)
    proj_1024(lambda ic: QF[:, ic * 256:(ic + 1) * 256])
    te.wait_ge(sSC, cSC.v)
    te.wait_ge(sVE, cVE.v)
    sc.wait_ge(sVE, cVE.v)
    for u in masked_units_for(0):
        u()
    masked_done_sc[0] = tvexp_box[0]
    sync.wait_ge(sVE, cVE.v)
    sync.dma_start(out=x2s_ap, in_=x_t[1]).then_inc(sLX, cLX.add(16))
    ve.wait_ge(sLX, cLX.v)
    ve.wait_ge(sTE, cTE.v)
    ve.wait_ge(sSC, cSC.v)
    ln_pair(LNTM[:, :], LNTMB[:, :], dum=PB, srcb=X2S)
    ln_ready_ve = cVE.v
    emit_lnfm_from_tm()
    te.wait_ge(sSC, cSC.v)
    te.wait_ge(sVE, m_ve[0])
    te.wait_ge(sSC, m_sc[0])
    proj_1024(lambda ic: QF[:, 2048 + ic * 256: 2048 + (ic + 1) * 256])

    for m in range(NQB):
        nkc = 2 * m + 2
        XTc = XT
        qbase = 2048 * (m % 2)
        tvo_m = 32 * (m % 2)
        if m > 0:
            sync.wait_ge(sVE, xt_free_ve)
            sync.dma_start(out=xt_ap, in_=x_t[m]).then_inc(sLD, cLD.add(16))
        q_done_sc = cSC.v
        te.wait_ge(sSC, q_done_sc)
        te.wait_ge(sVE, cVE.v)
        sc.wait_ge(sVE, cVE.v)
        if m + 2 < NQB:
            sync.wait_ge(sVE, cVE.v)
            sync.wait_ge(sST, cST.v)
            sync.dma_start(out=x2s_ap, in_=x_t[m + 2]).then_inc(sLX, cLX.add(16))
        causal_exp_sc = 0
        runmax_ve = [cVE.v, cVE.v]
        pb_pv_te = [cTE.v, cTE.v]
        num_free_sc = [0, 0]
        pend = None
        cmaxred_ve = 0
        mhtrans_te = 0

        def emit_pending():
            nonlocal pend, cmaxred_ve, mhtrans_te
            if pend is None:
                return
            hpp, comb_mark = pend
            pend = None
            te.wait_ge(sVE, comb_mark)
            if cmaxred_ve:
                te.wait_ge(sVE, cmaxred_ve)
            for par in range(2):
                for qt in range(2):
                    te.transpose(PSB16[:, par * 256 + qt * 128: par * 256 + (qt + 1) * 128],
                                 MH[:, par * 512 + qt * 128: par * 512 + (qt + 1) * 128],
                                 IDBF[:, :])
            te.mark(sTE, cTE)
            mhtrans_te = cTE.v
            ve.wait_ge(sTE, cTE.v)
            for par in range(2):
                for qt in range(2):
                    col = (2 * hpp + par) * 2 + qt
                    ve.reduce_max(CMAX[:, col:col + 1],
                                  PSB16[:, par * 256 + qt * 128: par * 256 + (qt + 1) * 128],
                                  axis=AX.X)
            ve.mark(sVE, cVE)
            cmaxred_ve = cVE.v

        units_A = []
        for hp in range(8):
            ic = hp
            h_e, h_o = 2 * hp, 2 * hp + 1
            rhs_e = QF[0:64, qbase + ic * 256: qbase + ic * 256 + 256]
            rhs_o = QF[64:128, qbase + ic * 256: qbase + ic * 256 + 256]
            ngr = (nkc + 1) // 2
            qk_mark = {}
            exp_mark = {}
            vm_mark = {}
            pv_mark = {}

            def emit_qk(g, ic=ic, rhs_e=rhs_e, rhs_o=rhs_o, qk_mark=qk_mark):
                nonlocal causal_exp_sc
                kcs = list(range(g * 2, min(g * 2 + 2, nkc)))
                if causal_exp_sc:
                    te.wait_ge(sSC, causal_exp_sc)
                for j, kc in enumerate(kcs):
                    te.matmul(PSA[:, j * 256: (j + 1) * 256],
                              K_sb[0:64, ic * T + kc * 128: ic * T + (kc + 1) * 128],
                              rhs_e, start=True, stop=True, tile_position=(0, 0))
                    te.matmul(PSA[:, 512 + j * 256: 512 + (j + 1) * 256],
                              K_sb[64:128, ic * T + kc * 128: ic * T + (kc + 1) * 128],
                              rhs_o, start=True, stop=True, tile_position=(64, 0))
                te.mark(sTE, cTE)
                qk_mark[g] = cTE.v

            def emit_sv(g, qk_mark=qk_mark, exp_mark=exp_mark, vm_mark=vm_mark,
                        pv_mark=pv_mark):
                nonlocal causal_exp_sc
                kcs = list(range(g * 2, min(g * 2 + 2, nkc)))
                assert len(kcs) == 2
                half = g % 2
                sc.wait_ge(sTE, qk_mark[g])
                if pb_pv_te[half]:
                    sc.wait_ge(sTE, pb_pv_te[half])
                sc.wait_ge(sVE, runmax_ve[half])
                sc.activation(PB[:, half * 1024: half * 1024 + 1024],
                              PSA[:, 0:1024],
                              ACT.Exp, scale=CST[:, 2:3]).then_inc(sSC, cSC.add(1))
                exp_mark[g] = cSC.v
                causal_exp_sc = cSC.v
                ve.wait_ge(sSC, exp_mark[g])
                if g == m:
                    ve.tensor_mul(PB[:, half * 1024: half * 1024 + 512],
                                  PB[:, half * 1024: half * 1024 + 512],
                                  TRI[:, 0:512])
                    ve.tensor_mul(PB[:, half * 1024 + 512: half * 1024 + 1024],
                                  PB[:, half * 1024 + 512: half * 1024 + 1024],
                                  TRI[:, 0:512])
                    ve.mark(sVE, cVE)
                ve.wait_ge(sVE, cVE.v)
                if g == 0:
                    if mhtrans_te:
                        ve.wait_ge(sTE, mhtrans_te)
                    ve.tensor_copy(MH[:, 0:1024],
                                   PB[:, half * 1024: half * 1024 + 1024])
                else:
                    ve.tensor_max(MH[:, 0:1024], MH[:, 0:1024],
                                  PB[:, half * 1024: half * 1024 + 1024])
                ve.mark(sVE, cVE)
                vm_mark[g] = cVE.v
                runmax_ve[half] = cVE.v

            def emit_pv(g, ic=ic, h_e=h_e, h_o=h_o, vm_mark=vm_mark,
                        pv_mark=pv_mark):
                kcs = list(range(g * 2, min(g * 2 + 2, nkc)))
                half = g % 2
                te.wait_ge(sVE, vm_mark[g])
                if g == 0:
                    if num_free_sc[0]:
                        te.wait_ge(sSC, num_free_sc[0])
                    if num_free_sc[1]:
                        te.wait_ge(sSC, num_free_sc[1])
                for j, kc in enumerate(kcs):
                    te.matmul(PSN[0:65, 0:256],
                              V_sb[:, kc * (H * 65) + h_e * 65: kc * (H * 65) + h_e * 65 + 65],
                              PB[:, half * 1024 + j * 256: half * 1024 + (j + 1) * 256],
                              start=(kc == 0), stop=(kc == nkc - 1))
                    te.matmul(PSC[0:65, 0:256],
                              V_sb[:, kc * (H * 65) + h_o * 65: kc * (H * 65) + h_o * 65 + 65],
                              PB[:, half * 1024 + 512 + j * 256: half * 1024 + 512 + (j + 1) * 256],
                              start=(kc == 0), stop=(kc == nkc - 1))
                te.mark(sTE, cTE)
                pv_mark[g] = cTE.v
                pb_pv_te[half] = cTE.v
                return cTE.v

            def mk_lead(hp=hp, emit_qk=emit_qk, emit_sv=emit_sv):
                def u():
                    emit_pending()
                    emit_qk(0)
                    emit_sv(0)
                return u

            def mk_cu(g, ngr=ngr, emit_qk=emit_qk, emit_sv=emit_sv,
                      emit_pv=emit_pv, hp=hp, h_e=h_e, h_o=h_o,
                      pv_mark=pv_mark):
                def u():
                    if g + 1 < ngr:
                        emit_qk(g + 1)
                    pvm = emit_pv(g)
                    if g + 1 < ngr:
                        emit_sv(g + 1)
                    if g == ngr - 1:
                        sc.wait_ge(sTE, pvm)
                        sc.copy(NUMBUF[0:65, h_e * 256:(h_e + 1) * 256],
                                PSN[0:65, 0:256])
                        sc.copy(NUMBUF[0:65, h_o * 256:(h_o + 1) * 256],
                                PSC[0:65, 0:256]).then_inc(sSC, cSC.add(1))
                        num_free_sc[0] = cSC.v
                        num_free_sc[1] = cSC.v
                        ve.wait_ge(sVE, cVE.v)
                        ve.tensor_max(MH[:, 0:256], MH[:, 0:256], MH[:, 256:512])
                        ve.tensor_max(MH[:, 512:768], MH[:, 512:768],
                                      MH[:, 768:1024]).then_inc(sVE, cVE.add(1))
                        nonlocal pend
                        pend = (hp, cVE.v)
                return u

            units_A.append(mk_lead())
            for g in range(ngr):
                units_A.append(mk_cu(g))

        if m + 2 < NQB:
            def u_ln():
                nonlocal ln_ready_ve
                ve.wait_ge(sLX, cLX.v)
                ve.wait_ge(sTE, cTE.v)
                ve.wait_ge(sSH, cSH.v)
                ln_pair(LNTM[:, :], LNTMB[:, :], dum=ATTN, srcb=X2S)
                ln_ready_ve = cVE.v
            units_A.insert(min(3, len(units_A)), u_ln)

        units_B = masked_units_for(m + 1) if m + 1 < NQB else []
        i = j = 0
        nb_, na_ = len(units_B), len(units_A)
        while i < nb_ or j < na_:
            if j < na_ and (i >= nb_ or j * nb_ <= i * na_):
                units_A[j]()
                j += 1
            else:
                units_B[i]()
                i += 1
        emit_pending()
        if m + 1 < NQB:
            masked_done_sc[m + 1] = tvexp_box[0]

        ve.wait_ge(sSC, masked_done_sc[m])
        ve.wait_ge(sVE, cVE.v)
        ve.tensor_max(TV[:, tvo_m:tvo_m + 32], TV[:, tvo_m:tvo_m + 32],
                      CMAX[:, 0:32])
        ve.mark(sVE, cVE)
        te.wait_ge(sSC, cSC.v)
        te.wait_ge(sVE, cVE.v)
        for h in range(H):
            for qt in range(2):
                te.transpose(PSB16[:, 512 + (h * 2 + qt) * 2: 513 + (h * 2 + qt) * 2],
                             NUMBUF[64:65, h * 256 + qt * 128: h * 256 + (qt + 1) * 128],
                             IDBF[64:65, 64:65])
        te.mark(sTE, cTE)
        ve.wait_ge(sTE, cTE.v)
        den_ap = PSB16.ap().rearrange("p (c two) -> p c two", two=2)[:, 256:288, 0:1]
        tv_ap = TV.ap().rearrange("p (c one) -> p c one", one=1)[:, tvo_m:tvo_m + 32, 0:1]
        scr_ap = SCR.ap().rearrange("p (c one) -> p c one", one=1)[:, 48:80, 0:1]
        ve.scalar_tensor_tensor(scr_ap, tv_ap, EPS_ATTN,
                                den_ap, ALU.mult, ALU.add)
        ve.mark(sVE, cVE)
        ve.wait_ge(sVE, cVE.v)
        ve.reciprocal(SCR[:, 80:112], SCR[:, 48:80])
        ve.mark(sVE, cVE)
        recip_ve = cVE.v
        sc.wait_ge(sST, cST.v)
        sc.wait_ge(sVE, ln_ready_ve)
        ve.wait_ge(sST, cST.v)
        te.wait_ge(sVE, recip_ve)
        amult_ve = {}
        evac_sc = {}
        for rnd in range(4):
            if rnd > 0:
                te.wait_ge(sSC, cSC.v)
            for k in range(4):
                h = rnd * 4 + k
                for qt in range(2):
                    te.transpose(PSA[0:1, k * 256 + qt * 128: k * 256 + (qt + 1) * 128],
                                 SCR[:, 80 + h * 2 + qt:81 + h * 2 + qt], IDF[:, :])
            te.mark(sTE, cTE)
            rt_mark = cTE.v
            for k in range(4):
                h = rnd * 4 + k
                par = h % 2
                numps = PSN if par == 0 else PSC
                slot = (h // 2) % 2
                sslot = h % 2
                sc.wait_ge(sTE, rt_mark)
                if h >= 2:
                    sc.wait_ge(sTE, cTE.v)
                sc.copy(X2S[0:1, 1024 + sslot * 256: 1024 + (sslot + 1) * 256],
                        PSA[0:1, k * 256:(k + 1) * 256]).then_inc(sSC, cSC.add(1))
                evac_sc[h] = cSC.v
                te.wait_ge(sSC, evac_sc[h])
                if h >= 4 and (h - 4) in amult_ve:
                    te.wait_ge(sVE, amult_ve[h - 4])
                te.matmul(numps[0:64, slot * 256:(slot + 1) * 256],
                          ONESF[0:1, 0:64],
                          X2S[0:1, 1024 + sslot * 256: 1024 + (sslot + 1) * 256],
                          start=True, stop=True)
                te.mark(sTE, cTE)
                ve.wait_ge(sTE, cTE.v)
                if par == 0:
                    dst = ATTN[0:64, (h // 2) * 256:(h // 2 + 1) * 256]
                else:
                    dst = ATTS[0:64, (h // 2) * 256:(h // 2 + 1) * 256]
                ve.scalar_tensor_tensor(dst, NUMBUF[0:64, h * 256:(h + 1) * 256], 1.0,
                                        numps[0:64, slot * 256:(slot + 1) * 256],
                                        ALU.mult, ALU.mult).then_inc(sVE, cVE.add(1))
                amult_ve[h] = cVE.v
                if par == 1:
                    sync.wait_ge(sVE, amult_ve[h])
                    sync.dma_start(out=ATTN[64:128, (h // 2) * 256:(h // 2 + 1) * 256],
                                   in_=ATTS[0:64, (h // 2) * 256:(h // 2 + 1) * 256]
                                   ).then_inc(sSH, cSH.add(16))

        if m + 2 < NQB:
            emit_lnfm_from_tm()
            te.wait_ge(sSC, cSC.v)
            te.wait_ge(sGD, cGD.v)
            te.wait_ge(sVE, m_ve[0])
            te.wait_ge(sSC, m_sc[0])
            proj_1024(lambda ic: QF[:, qbase + ic * 256: qbase + (ic + 1) * 256])
        gp.wait_ge(sTE, cTE.v)
        gp.dma_start(out=WB.ap().rearrange("p (ic d) -> p ic d", ic=8),
                     in_=wo_d.rearrange("(ic p) d -> p ic d", p=128)).then_inc(sGD, cGD.add(16))

        te.wait_ge(sSH, cSH.v)
        te.wait_ge(sVE, cVE.v)
        te.wait_ge(sGD, cGD.v)
        te.wait_ge(sSC, cSC.v)
        te.wait_ge(sVE, m_ve[0])
        te.wait_ge(sSC, m_sc[0])
        for sub in range(2):
            for dh2 in range(2):
                for icc in range(8):
                    te.matmul(PSA[:, sub * 1024 + dh2 * 512: sub * 1024 + (dh2 + 1) * 512],
                              ATTN[:, icc * 256 + sub * 128: icc * 256 + (sub + 1) * 128],
                              WB[:, icc * D + dh2 * 512: icc * D + (dh2 + 1) * 512],
                              start=(icc == 0), stop=(icc == 7))
        te.mark(sTE, cTE)
        ve.wait_ge(sTE, cTE.v)
        ve.wait_ge(sLD, cLD.v)
        ve.wait_ge(sST, cST.v)
        ve.tensor_add(XTc[:, 0:D], XTc[:, 0:D], BOB[:, :])
        ve.tensor_add(XTc[:, D:2 * D], XTc[:, D:2 * D], BOB[:, :]).then_inc(sVE, cVE.add(1))
        ve.wait_ge(sVE, cVE.v)
        for sub in range(2):
            for dh2 in range(2):
                ve.scalar_tensor_tensor(
                    X2S[:, sub * 1024 + dh2 * 512: sub * 1024 + (dh2 + 1) * 512],
                    PSA[:, sub * 1024 + dh2 * 512: sub * 1024 + (dh2 + 1) * 512],
                    1.0, XTc[:, sub * D + dh2 * 512: sub * D + (dh2 + 1) * 512],
                    ALU.mult, ALU.add)
        ve.mark(sVE, cVE)
        xt_free_ve = cVE.v
        sync.wait_ge(sVE, cVE.v)
        sync.dma_start(out=x2_t[m], in_=x2s_ap).then_inc(sST, cST.add(16))
        if m + 3 < NQB:
            gp.wait_ge(sTE, cTE.v)
            gp.dma_start(out=WB.ap().rearrange("p (dc i) -> p dc i", dc=8),
                         in_=wq_d.rearrange("(dc p) i -> p dc i", p=128)
                         ).then_inc(sGD, cGD.add(16))

    nc.all_engine_barrier()

    wet_r = wet_d.rearrange("e (hc p) f -> p e hc f", p=128)
    wt_r = WT.ap().rearrange("p (e hc f) -> p e hc f", e=E, hc=8)
    for e in range(E):
        gp.dma_start(out=wt_r[:, e], in_=wet_r[:, e]).then_inc(sGD, cGD.add(16))
    a2_store = cST.v
    bsets = [(XT, LN2TM, LN2F32, MBC, sLD, cLD),
             (XTB2, LN2TMb, LN2F32, MBC, sLX, cLX)]
    hdr_ve = {}
    hdr_te = {}
    mask_ve = {}
    fin_ve = {}

    def hdrA(tt):
        XTb, TMb, F32b, MBCb_, sem, ct = bsets[tt % 2]
        if tt >= 2:
            sync.wait_ge(sVE, fin_ve.get(tt - 2, 0))
        sync.wait_ge(sST, a2_store)
        sync.dma_start(out=XTb.ap().rearrange("p (s c) -> p s c", s=2),
                       in_=x2_t[tt]).then_inc(sem, ct.add(16))
        ve.wait_ge(sem, ct.v)
        if tt >= 2:
            ve.wait_ge(sVE, cVE.v)
        ln_pair(TMb[:, 0:D], TMb[:, D:2 * D], dum=DUMB, srcb=XTb)
        hdr_ve[tt] = cVE.v

    def hdrB(tt):
        XTb, TMb, F32b, MBCb_, sem, ct = bsets[tt % 2]
        te.wait_ge(sVE, hdr_ve[tt])
        for sub in range(2):
            src = TMb[:, sub * D:(sub + 1) * D]
            for rnd in range(2):
                te.wait_ge(sSC, cSC.v)
                for c in range(4):
                    dc = rnd * 4 + c
                    te.transpose(PSB[:, c * 128:(c + 1) * 128],
                                 src[:, dc * 128:(dc + 1) * 128], IDF[:, :])
                te.mark(sTE, cTE)
                sc.wait_ge(sTE, cTE.v)
                dst = F32b.ap().rearrange("p (dc t) -> p dc t", dc=8)[
                    :, rnd * 4:(rnd + 1) * 4, sub * 128:(sub + 1) * 128]
                sc.copy(dst, PSB.ap().rearrange("p (c t) -> p c t", c=4)
                        ).then_inc(sSC, cSC.add(1))
        te.wait_ge(sSC, cSC.v)
        for sub in range(2):
            for hc in range(8):
                te.matmul(PSB[:, sub * 8:(sub + 1) * 8],
                          F32b[:, hc * 256 + sub * 128: hc * 256 + (sub + 1) * 128],
                          GWT[:, hc * 8:(hc + 1) * 8],
                          start=(hc == 0), stop=(hc == 7))
        te.mark(sTE, cTE)
        ve.wait_ge(sTE, cTE.v)
        if tt >= 1:
            ve.wait_ge(sVE, cVE.v)
        for sub in range(2):
            ve.reduce_max(MQ[:, sub:sub + 1], PSB[:, sub * 8:(sub + 1) * 8], axis=AX.X)
        ve.mark(sVE, cVE)
        ve.wait_ge(sVE, cVE.v)
        for sub in range(2):
            ve.tensor_scalar(EQ8[:, sub * 8:(sub + 1) * 8], PSB[:, sub * 8:(sub + 1) * 8],
                             MQ[:, sub:sub + 1], None, ALU.is_equal)
        ve.mark(sVE, cVE)
        ve.wait_ge(sVE, cVE.v)
        for sub in range(2):
            ve.scalar_tensor_tensor(SCR[:, 32 + sub * 8: 40 + sub * 8],
                                    EQ8[:, sub * 8:(sub + 1) * 8], -16.0,
                                    IOTA[:, 8:16], ALU.mult, ALU.add)
        ve.mark(sVE, cVE)
        ve.wait_ge(sVE, cVE.v)
        for sub in range(2):
            ve.tensor_reduce(IDX[:, sub:sub + 1], SCR[:, 32 + sub * 8: 40 + sub * 8],
                             AX.X, ALU.min)
        ve.mark(sVE, cVE)
        te.wait_ge(sVE, cVE.v)
        for sub in range(2):
            te.transpose(PSB[0:1, 16 + sub * 128: 16 + (sub + 1) * 128],
                         IDX[:, sub:sub + 1], IDF[:, :])
        te.mark(sTE, cTE)
        sc.wait_ge(sTE, cTE.v)
        sc.wait_ge(sST, cST.v)
        sc.copy(X2S[0:1, 1024:1280], PSB[0:1, 16:272]).then_inc(sSC, cSC.add(1))
        te.wait_ge(sSC, cSC.v)
        if tt >= 1:
            te.wait_ge(sVE, mask_ve.get(tt - 1, 0))
        te.matmul(PSC[:, 0:256], ONESF[0:1, 0:128], X2S[0:1, 1024:1280],
                  start=True, stop=True)
        te.mark(sTE, cTE)
        ve.wait_ge(sTE, cTE.v)
        ve.wait_ge(sVE, cVE.v)
        for e in range(E):
            ve.tensor_scalar(MBCb_[:, e * 256:(e + 1) * 256], PSC[:, 0:256],
                             float(e), None, ALU.is_equal)
        ve.mark(sVE, cVE)
        mask_ve[tt] = cVE.v
        hdr_te[tt] = cTE.v

    hdrA(0)
    hdrB(0)
    for tt in range(NQB):
        XTb, TMb, F32b, MBCb_, _, _ = bsets[tt % 2]
        if tt + 1 < NQB:
            hdrA(tt + 1)
        if tt == 0:
            te.wait_ge(sGD, cGD.v)
        NEH = E * 8
        mln_ve = {}
        mm_te = {}

        def emit_mln(i):
            e, hc = i // 8, i % 8
            buf = i % 2
            if (i - 2) in mm_te:
                ve.wait_ge(sTE, mm_te[i - 2])
            ve.tensor_mul(MLN[:, buf * 256:(buf + 1) * 256],
                          F32b[:, hc * 256:(hc + 1) * 256],
                          MBCb_[:, e * 256:(e + 1) * 256]).then_inc(sVE, cVE.add(1))
            mln_ve[i] = cVE.v

        ve.wait_ge(sVE, mask_ve[tt])
        ve.wait_ge(sTE, cTE.v)
        emit_mln(0)
        for i in range(NEH):
            if i + 1 < NEH:
                emit_mln(i + 1)
            buf = i % 2
            te.wait_ge(sVE, mln_ve[i])
            if i == 0:
                te.wait_ge(sVE, fin_ve.get(tt - 1, 0))
            for sub in range(2):
                for fh in range(2):
                    te.matmul(PSA[:, sub * 1024 + fh * 512: sub * 1024 + (fh + 1) * 512],
                              MLN[:, buf * 256 + sub * 128: buf * 256 + (sub + 1) * 128],
                              WT[:, i * D + fh * 512: i * D + (fh + 1) * 512],
                              start=(i == 0), stop=(i == NEH - 1))
            te.mark(sTE, cTE)
            mm_te[i] = cTE.v
        ve.wait_ge(sTE, cTE.v)
        ve.wait_ge(sST, cST.v)
        for sub in range(2):
            for fh in range(2):
                ve.scalar_tensor_tensor(
                    X2S[:, sub * 1024 + fh * 512: sub * 1024 + (fh + 1) * 512],
                    PSA[:, sub * 1024 + fh * 512: sub * 1024 + (fh + 1) * 512],
                    1.0, TMb[:, sub * D + fh * 512: sub * D + (fh + 1) * 512],
                    ALU.mult, ALU.add)
        ve.mark(sVE, cVE)
        ve.wait_ge(sVE, cVE.v)
        for sub in range(2):
            for fh in range(2):
                ve.tensor_add(
                    X2S[:, sub * 1024 + fh * 512: sub * 1024 + (fh + 1) * 512],
                    X2S[:, sub * 1024 + fh * 512: sub * 1024 + (fh + 1) * 512],
                    XTb[:, sub * D + fh * 512: sub * D + (fh + 1) * 512])
        ve.mark(sVE, cVE)
        fin_ve[tt] = cVE.v
        sync.wait_ge(sVE, cVE.v)
        sync.dma_start(out=out_t[tt], in_=x2s_ap).then_inc(sST, cST.add(16))
        if tt + 1 < NQB:
            hdrB(tt + 1)

    sync.wait_ge(sST, cST.v)
    if dbg is not None:
        dbg.update(locals())
    return nc


def _consts():
    ident_bf = np.eye(128, dtype=ml_dtypes.bfloat16)
    ident_f = np.eye(128, dtype=np.float32)
    k = np.arange(128)[:, None]
    q = np.arange(128)[None, :]
    tri = (k <= q).astype(np.float32)
    tri0 = np.concatenate([tri, np.ones((128, 128), np.float32)], axis=1)
    tri1 = np.concatenate([np.zeros((128, 128), np.float32), tri], axis=1)
    tri_all = np.concatenate([tri0, tri1], axis=1).astype(ml_dtypes.bfloat16)
    iota = np.broadcast_to(np.arange(8, dtype=np.float32), (128, 8))
    iota16 = np.concatenate([iota, iota + 16.0], axis=1).astype(np.float32)
    ones_f = np.ones((1, 128), dtype=np.float32)
    cst = np.zeros((128, 8), dtype=np.float32)
    cst[:, 0] = EPS_LN
    cst[:, 1] = -0.5
    cst[:, 2] = SCALE
    return ident_bf, ident_f, tri_all, np.ascontiguousarray(iota16), ones_f, cst


def kernel(L0_attn_x, L0_attn_Wq, L0_attn_Wkv, L0_attn_Wo, L0_attn_bo,
           L0_ffn_gate_w, L0_ffn_expert_w):
    x = np.ascontiguousarray(np.asarray(L0_attn_x, dtype=np.float32))
    wq = np.ascontiguousarray(np.asarray(L0_attn_Wq, dtype=np.float32))
    wkv = np.ascontiguousarray(np.asarray(L0_attn_Wkv, dtype=np.float32))
    wo = np.ascontiguousarray(np.asarray(L0_attn_Wo, dtype=np.float32))
    bo = np.asarray(L0_attn_bo, dtype=np.float32)
    gw = np.asarray(L0_ffn_gate_w, dtype=np.float32)
    we = np.asarray(L0_ffn_expert_w, dtype=np.float32)
    B, T, _ = x.shape
    bo_b = np.ascontiguousarray(np.broadcast_to(bo[None, :], (128, D)))
    gwt = np.ascontiguousarray(gw.T)
    wet = np.ascontiguousarray(we.transpose(0, 2, 1))
    ident_bf, ident_f, tri_all, iota16, ones_f, cst = _consts()

    nc = build(T=T)
    shared = {
        "wq": wq, "wkv": wkv, "wo": wo, "bo_b": bo_b, "gwt": gwt, "wet": wet,
        "id_bf": ident_bf, "id_f32": ident_f, "tri": tri_all,
        "iota8": iota16, "ones_f": ones_f, "cst": cst,
    }
    in_maps = [dict(shared, x=np.ascontiguousarray(x[b])) for b in range(B)]
    res = run_bass_kernel_spmd(nc, in_maps, core_ids=list(range(B)))
    out = np.stack([np.asarray(r["out"]) for r in res.results], axis=0)
    return out.astype(np.float32)


# version 18
# speedup vs baseline: 1.0517x; 1.0517x over previous
import numpy as np
import ml_dtypes
import concourse.bass as bass
import concourse.mybir as mybir
from concourse.bass_utils import run_bass_kernel_spmd

F32 = mybir.dt.float32
BF16 = mybir.dt.bfloat16
AX = mybir.AxisListType
ALU = mybir.AluOpType
ACT = mybir.ActivationFunctionType

D = 1024
H = 16
DH = 64
E = 8
SCALE = DH ** -0.5
EPS_ATTN = 1e-8
EPS_LN = 1e-5



class EngW:

    def __init__(self, eng):
        self._e = eng
        self.last = None

    def __getattr__(self, name):
        f = getattr(self._e, name)
        if not callable(f):
            return f
        rec = name not in ("wait_ge", "nop")

        def w(*a, **k):
            r = f(*a, **k)
            if rec and hasattr(r, "then_inc"):
                self.last = r
            return r
        return w

    def mark(self, sem, ct):
        self.last.then_inc(sem, ct.add(1))


class Ct:
    def __init__(self):
        self.v = 0

    def add(self, n):
        self.v += n
        return n


def build(T=4096, dbg=None):
    QB = 256
    NQB = T // QB
    NKC = T // 128

    nc = bass.Bass("TRN2", target_bir_lowering=False)

    x_d = nc.dram_tensor("x", [T, D], F32, kind="ExternalInput").ap()
    wq_d = nc.dram_tensor("wq", [D, D], F32, kind="ExternalInput").ap()
    wkv_d = nc.dram_tensor("wkv", [D, 2 * D], F32, kind="ExternalInput").ap()
    wo_d = nc.dram_tensor("wo", [D, D], F32, kind="ExternalInput").ap()
    bo_d = nc.dram_tensor("bo_b", [128, D], F32, kind="ExternalInput").ap()
    gwt_d = nc.dram_tensor("gwt", [D, E], F32, kind="ExternalInput").ap()
    wet_d = nc.dram_tensor("wet", [E, D, D], F32, kind="ExternalInput").ap()
    idbf_d = nc.dram_tensor("id_bf", [128, 128], BF16, kind="ExternalInput").ap()
    idf32_d = nc.dram_tensor("id_f32", [128, 128], F32, kind="ExternalInput").ap()
    tri_d = nc.dram_tensor("tri", [128, 512], BF16, kind="ExternalInput").ap()
    iota_d = nc.dram_tensor("iota8", [128, 16], F32, kind="ExternalInput").ap()
    ones_d = nc.dram_tensor("ones_f", [1, 128], F32, kind="ExternalInput").ap()
    cst_d = nc.dram_tensor("cst", [128, 8], F32, kind="ExternalInput").ap()
    out_d = nc.dram_tensor("out", [T, D], F32, kind="ExternalOutput").ap()
    x2_d = nc.dram_tensor("x2scr", [T, D], F32, kind="Internal").ap()

    sLD = nc.alloc_semaphore("sLD")
    sLX = nc.alloc_semaphore("sLX")
    sLC = nc.alloc_semaphore("sLC")
    sGD = nc.alloc_semaphore("sGD")
    sST = nc.alloc_semaphore("sST")
    sSH = nc.alloc_semaphore("sSH")
    sTE = nc.alloc_semaphore("sTE")
    sSC = nc.alloc_semaphore("sSC")
    sVE = nc.alloc_semaphore("sVE")
    sGP = nc.alloc_semaphore("sGP")
    cLD, cLX, cLC, cGD, cST, cSH, cTE, cSC, cVE, cGP = (Ct() for _ in range(10))

    ARENA_BYTES = 210800
    _arena = nc.alloc_sbuf_tensor("ARENA", [128, ARENA_BYTES], mybir.dt.uint8)
    _base = nc.lookup_mloc(_arena).addr
    _off = [_base]

    def _at(name, shape, dtype, region_end=None):
        nbytes = shape[1] * (2 if dtype == BF16 else 4)
        addr = (_off[0] + 31) & ~31
        h = nc.alloc_sbuf_tensor_at(name, shape, dtype, offset=addr)
        _off[0] = addr + nbytes
        return h

    XT = _at("XT", [128, 2 * D], F32)
    _x2s_off = (_off[0] + 31) & ~31
    X2S = _at("X2S", [128, 2048], F32)
    SCR = _at("SCR", [128, 128], F32)
    BOB = _at("BOB", [128, D], BF16)
    IDBF = _at("IDBF", [128, 128], BF16)
    IDF = _at("IDF", [128, 128], F32)
    TRI = _at("TRI", [128, 512], BF16)
    IOTA = _at("IOTA", [128, 16], F32)
    ONESF = _at("ONESF", [1, 128], F32)
    CST = _at("CST", [128, 8], F32)
    GWT = _at("GWT", [128, 64], F32)
    MQ = _at("MQ", [128, 4], F32)
    EQ8 = _at("EQ8", [128, 16], F32)
    IDX = _at("IDX", [128, 2], F32)
    TV = _at("TV", [128, 32], F32)
    PMAX = _at("PMAX", [128, 16], F32)
    MH = _at("MH", [128, 1024], BF16)
    CMAX = _at("CMAX", [128, 32], F32)
    LNTM = _at("LNTM", [128, D], BF16)
    LNTMB = _at("LNTMB", [128, D], BF16)
    ATTS = nc.alloc_sbuf_tensor_at("ATTS", [128, 2048], BF16, offset=_x2s_off)
    MLN = _at("MLN", [128, 512], BF16)
    shared_end = _off[0]
    _k_off = (_off[0] + 31) & ~31
    WB = _at("WB", [128, 8 * D], BF16)
    K_sb = _at("K", [128, 8 * T], BF16)
    V_sb = _at("V", [128, NKC * (H * 65)], BF16)
    QF = _at("QF", [128, 2048], BF16)
    PB = _at("PB", [128, 2048], BF16)
    ATTN = _at("ATTN", [128, 2048], BF16)
    LNFM = _at("LNFM", [128, 2048], BF16)
    NUMBUF = _at("NUMBUF", [128, 4096], BF16)
    a_end = _off[0]
    _off[0] = shared_end
    WT = _at("WT", [128, 64 * D], BF16)
    LN2TM = _at("LN2TM", [128, 2048], F32)
    LN2F32 = _at("LN2F32", [128, 2048], F32)
    MBC = _at("MBC", [128, 2048], BF16)
    XTB2 = _at("XTB2", [128, 2048], F32)
    LN2TMb = _at("LN2TMb", [128, 2048], F32)
    DUMB = _at("DUMB", [128, 2048], BF16)
    b_end = _off[0]
    assert max(a_end, b_end) < _base + ARENA_BYTES, (_base, shared_end, a_end, b_end)
    LN2F16 = LN2F32
    PSA = nc.alloc_psum_tensor("PSA", [128, 2048], F32)
    PSB = nc.alloc_psum_tensor("PSB", [128, 512], F32)
    PSB16 = nc.alloc_psum_tensor("PSB16", [128, 1024], BF16)
    PSN = nc.alloc_psum_tensor("PSN", [128, 512], F32)
    PSC = nc.alloc_psum_tensor("PSC", [128, 512], F32)

    sync, gp, te, sc, ve = (EngW(nc.sync), EngW(nc.gpsimd), EngW(nc.tensor),
                            EngW(nc.scalar), EngW(nc.vector))

    sync.dma_start(out=IDBF[:, :], in_=idbf_d).then_inc(sLC, cLC.add(16))
    sync.dma_start(out=IDF[:, :], in_=idf32_d).then_inc(sLC, cLC.add(16))
    sync.dma_start(out=TRI[:, :], in_=tri_d).then_inc(sLC, cLC.add(16))
    sync.dma_start(out=IOTA[:, :], in_=iota_d).then_inc(sLC, cLC.add(16))
    sync.dma_start(out=ONESF[:, :], in_=ones_d).then_inc(sLC, cLC.add(16))
    sync.dma_start(out=CST[:, :], in_=cst_d).then_inc(sLC, cLC.add(16))
    gp.dma_start(out=BOB[:, :], in_=bo_d).then_inc(sGD, cGD.add(16))
    sync.dma_start(out=GWT.ap().rearrange("p (hc e) -> p hc e", hc=8),
                   in_=gwt_d.rearrange("(hc p) e -> p hc e", p=128)).then_inc(sLC, cLC.add(16))
    gp.dma_start(out=WB.ap().rearrange("p (dc i) -> p dc i", dc=8),
                 in_=wkv_d.rearrange("(dc p) i -> p dc i", p=128)[:, :, 0:D]
                 ).then_inc(sGD, cGD.add(16))
    ve.wait_ge(sLC, cLC.v)
    sc.wait_ge(sLC, cLC.v)
    te.wait_ge(sLC, cLC.v)
    ve.memset(V_sb.ap().rearrange("p (k h j) -> p k h j", k=NKC, h=H)[:, :, :, 64:65], 1.0)
    ve.mark(sVE, cVE)

    x_t = x_d.rearrange("(n s p) c -> n p s c", s=2, p=128)
    x2_t = x2_d.rearrange("(n s p) c -> n p s c", s=2, p=128)
    out_t = out_d.rearrange("(n s p) c -> n p s c", s=2, p=128)
    xt_ap = XT.ap().rearrange("p (s c) -> p s c", s=2)
    x2s_ap = X2S.ap().rearrange("p (s c) -> p s c", s=2)

    def ln_pair(out0, out1, dum=None, srcb=None):
        cols = [SCR[:, c:c + 1] for c in range(16)]
        (s_sum0, s_ss0, s_m0, s_e0, s_v0, s_l0, s_r0, _,
         s_sum1, s_ss1, s_m1, s_e1, s_v1, s_l1, s_r1, _) = cols
        if srcb is None:
            srcb = XT
        src0 = srcb[:, 0:D]
        src1 = srcb[:, D:2 * D]
        ve.reduce_sum(s_sum0, src0, axis=AX.X)
        ve.reduce_sum(s_sum1, src1, axis=AX.X)
        ve.scalar_tensor_tensor(dum[:, 0:D], src0, 1.0, src0, ALU.mult, ALU.mult,
                                accum_out=s_ss0)
        ve.scalar_tensor_tensor(dum[:, D:2 * D], src1, 1.0, src1, ALU.mult, ALU.mult,
                                accum_out=s_ss1).then_inc(sVE, cVE.add(1))
        ve.wait_ge(sVE, cVE.v)
        ve.tensor_scalar_mul(s_m0, s_sum0, 1.0 / D)
        ve.tensor_scalar_mul(s_m1, s_sum1, 1.0 / D)
        ve.tensor_scalar_mul(s_e0, s_ss0, 1.0 / D)
        ve.tensor_scalar_mul(s_e1, s_ss1, 1.0 / D).then_inc(sVE, cVE.add(1))
        ve.wait_ge(sVE, cVE.v)
        ve.tensor_mul(s_v0, s_m0, s_m0)
        ve.tensor_mul(s_v1, s_m1, s_m1).then_inc(sVE, cVE.add(1))
        ve.wait_ge(sVE, cVE.v)
        ve.tensor_sub(s_v0, s_e0, s_v0)
        ve.tensor_sub(s_v1, s_e1, s_v1).then_inc(sVE, cVE.add(1))
        sc.wait_ge(sVE, cVE.v)
        sc.activation(s_l0, s_v0, ACT.Ln, bias=CST[:, 0:1])
        sc.activation(s_l1, s_v1, ACT.Ln, bias=CST[:, 0:1]).then_inc(sSC, cSC.add(1))
        sc.wait_ge(sSC, cSC.v)
        sc.activation(s_r0, s_l0, ACT.Exp, scale=CST[:, 1:2])
        sc.activation(s_r1, s_l1, ACT.Exp, scale=CST[:, 1:2]).then_inc(sSC, cSC.add(1))
        ve.wait_ge(sSC, cSC.v)
        ve.tensor_scalar(out0, src0, s_m0[:, 0:1], s_r0[:, 0:1], ALU.subtract, ALU.mult)
        ve.tensor_scalar(out1, src1, s_m1[:, 0:1], s_r1[:, 0:1],
                         ALU.subtract, ALU.mult).then_inc(sVE, cVE.add(1))


    def emit_ln_to_fm(dst_fm, f32=False, dst32=None, srcb=None):
        if f32:
            ln_pair(dst32[:, 0:D], dst32[:, D:2 * D], dum=DUMB, srcb=srcb)
            src0, src1 = dst32[:, 0:D], dst32[:, D:2 * D]
        else:
            ln_pair(LNTM[:, :], LNTMB[:, :], dum=PB, srcb=srcb)
            src0, src1 = LNTM[:, :], LNTMB[:, :]
        te.wait_ge(sVE, cVE.v)
        for sub, src in ((0, src0), (1, src1)):
            if f32:
                for rnd in range(2):
                    te.wait_ge(sSC, cSC.v)
                    for c in range(4):
                        dc = rnd * 4 + c
                        te.transpose(PSB[:, c * 128:(c + 1) * 128],
                                     src[:, dc * 128:(dc + 1) * 128], IDF[:, :])
                    te.mark(sTE, cTE)
                    sc.wait_ge(sTE, cTE.v)
                    dst = dst_fm.ap().rearrange("p (dc t) -> p dc t", dc=8)[
                        :, rnd * 4:(rnd + 1) * 4, sub * 128:(sub + 1) * 128]
                    sc.copy(dst, PSB.ap().rearrange("p (c t) -> p c t", c=4)
                            ).then_inc(sSC, cSC.add(1))
            else:
                te.wait_ge(sSC, cSC.v)
                for c in range(8):
                    te.transpose(PSB16[:, c * 128:(c + 1) * 128],
                                 src[:, c * 128:(c + 1) * 128], IDBF[:, :])
                te.mark(sTE, cTE)
                sc.wait_ge(sTE, cTE.v)
                dst = dst_fm.ap().rearrange("p (dc t) -> p dc t", dc=8)[
                    :, :, sub * 128:(sub + 1) * 128]
                sc.copy(dst, PSB16.ap().rearrange("p (dc t) -> p dc t", dc=8)
                        ).then_inc(sSC, cSC.add(1))
        te.wait_ge(sSC, cSC.v)

    def proj_1024(dst_sb, rhs_cols=256):
        marks = []
        for ic in range(8):
            bank = ic % 4
            if ic >= 4:
                te.wait_ge(sSC, cSC.v)
            for dc in range(8):
                te.matmul(PSA[:, bank * 512:bank * 512 + 256],
                          WB[:, dc * D + ic * 128:dc * D + (ic + 1) * 128],
                          LNFM[:, dc * 256:(dc + 1) * 256],
                          start=(dc == 0), stop=(dc == 7))
            te.mark(sTE, cTE)
            marks.append(cTE.v)
            sc.wait_ge(sTE, cTE.v)
            sc.copy(dst_sb(ic), PSA[:, bank * 512:bank * 512 + 256]
                    ).then_inc(sSC, cSC.add(1))
        return marks

    for pas in (0, 1):
        if pas == 1:
            gp.wait_ge(sTE, cTE.v)
            gp.dma_start(out=WB.ap().rearrange("p (dc i) -> p dc i", dc=8),
                         in_=wkv_d.rearrange("(dc p) i -> p dc i", p=128)[:, :, D:2 * D]
                         ).then_inc(sGD, cGD.add(16))
        for tt in range(NQB):
            sync.wait_ge(sVE, cVE.v)
            sync.dma_start(out=xt_ap, in_=x_t[tt]).then_inc(sLD, cLD.add(16))
            ve.wait_ge(sLD, cLD.v)
            if pas == 0 and tt == 0:
                ve.wait_ge(sGD, cGD.v)
            emit_ln_to_fm(LNFM)
            te.wait_ge(sGD, cGD.v)
            if pas == 0:
                proj_1024(lambda ic, _tt=tt: K_sb[:, ic * T + _tt * 256: ic * T + (_tt + 1) * 256])
            else:
                for tc in range(2):
                    half = tc % 2
                    te.wait_ge(sSC, cSC.v)
                    for dc in range(8):
                        lhs = LNFM[:, dc * 256 + tc * 128: dc * 256 + (tc + 1) * 128]
                        te.matmul(PSA[:, half * 1024:half * 1024 + 512],
                                  lhs, WB[:, dc * D:dc * D + 512],
                                  start=(dc == 0), stop=(dc == 7))
                        te.matmul(PSA[:, half * 1024 + 512:half * 1024 + 1024],
                                  lhs, WB[:, dc * D + 512:dc * D + 1024],
                                  start=(dc == 0), stop=(dc == 7))
                    te.mark(sTE, cTE)
                    kc = tt * 2 + tc
                    sc.wait_ge(sTE, cTE.v)
                    sc.copy(V_sb.ap().rearrange("p (k h j) -> p k h j", k=NKC, h=H)
                            [:, kc, :, 0:64],
                            PSA.ap().rearrange("p (g h j) -> p g h j", g=2, h=H)[:, half]
                            ).then_inc(sSC, cSC.add(1))
            ve.wait_ge(sSC, cSC.v)
            ve.memset(MQ[:, 3:4], 0.0)
            ve.mark(sVE, cVE)

    nc.all_engine_barrier()

    xt_free_ve = 0
    ln_ready_ve = 0
    wb_free_te = 0
    gp.wait_ge(sTE, cTE.v)
    gp.dma_start(out=WB.ap().rearrange("p (dc i) -> p dc i", dc=8),
                 in_=wq_d.rearrange("(dc p) i -> p dc i", p=128)).then_inc(sGD, cGD.add(16))
    wq_gd = cGD.v
    sync.wait_ge(sVE, cVE.v)
    sync.dma_start(out=xt_ap, in_=x_t[0]).then_inc(sLD, cLD.add(16))
    ve.wait_ge(sLD, cLD.v)
    emit_ln_to_fm(LNFM, srcb=XT)
    for m in range(NQB):
        nkc = 2 * m + 2
        XTc = XT
        if m == 0:
            te.wait_ge(sGD, wq_gd)
            proj_1024(lambda ic: QF[:, ic * 256:(ic + 1) * 256])
        else:
            sync.wait_ge(sVE, xt_free_ve)
            sync.dma_start(out=xt_ap, in_=x_t[m]).then_inc(sLD, cLD.add(16))
        q_done_sc = cSC.v

        te.wait_ge(sSC, q_done_sc)
        te.wait_ge(sVE, cVE.v)
        sc.wait_ge(sVE, cVE.v)
        mhalf_ve = [0, 0]
        mhalf_sc = [0, 0]
        tvexp_sc = 0
        odd_scr_ve = [0, 0]
        ph = 0
        for hp in range(8):
            ic = hp
            h_e, h_o = 2 * hp, 2 * hp + 1
            for qt in range(2):
                qtg = m * 2 + qt
                k0 = 128 * qtg
                npc = (T - k0 + 511) // 512
                w0 = min(512, T - k0)
                lhs_e = QF[0:64, ic * 256 + qt * 128: ic * 256 + (qt + 1) * 128]
                lhs_o = QF[64:128, ic * 256 + qt * 128: ic * 256 + (qt + 1) * 128]
                for pz in range(npc):
                    a = k0 + pz * 512
                    b = min(a + 512, T)
                    half = ph % 2
                    ph += 1
                    if mhalf_ve[half]:
                        te.wait_ge(sVE, mhalf_ve[half])
                    if mhalf_sc[half]:
                        te.wait_ge(sSC, mhalf_sc[half])
                    te.matmul(PSA[:, half * 1024: half * 1024 + (b - a)],
                              lhs_e, K_sb[0:64, ic * T + a: ic * T + b],
                              start=True, stop=True, tile_position=(0, 0))
                    te.matmul(PSA[:, half * 1024 + 512: half * 1024 + 512 + (b - a)],
                              lhs_o, K_sb[64:128, ic * T + a: ic * T + b],
                              start=True, stop=True, tile_position=(64, 0))
                    te.mark(sTE, cTE)
                    ve.wait_ge(sTE, cTE.v)
                    if pz == 0:
                        ve.wait_ge(sVE, cVE.v)
                    ve.reduce_max(PMAX[:, qt * 8 + pz:qt * 8 + pz + 1],
                                  PSA[:, half * 1024: half * 1024 + (b - a)],
                                  axis=AX.X).then_inc(sVE, cVE.add(1))
                    mhalf_ve[half] = cVE.v
                    spar = pz % 2
                    scr_o = spar * 512
                    sc.wait_ge(sTE, cTE.v)
                    if odd_scr_ve[spar]:
                        sc.wait_ge(sVE, odd_scr_ve[spar])
                    sc.activation(PB[:, scr_o:scr_o + (b - a)],
                                  PSA[:, half * 1024 + 512: half * 1024 + 512 + (b - a)],
                                  ACT.Exp, scale=CST[:, 2:3]).then_inc(sSC, cSC.add(1))
                    mhalf_sc[half] = cSC.v
                    ve.wait_ge(sSC, cSC.v)
                    ve.wait_ge(sVE, cVE.v)
                    if pz == 0:
                        ve.tensor_copy(PB[:, 1536:1536 + (b - a)],
                                       PB[:, scr_o:scr_o + (b - a)]
                                       ).then_inc(sVE, cVE.add(1))
                    else:
                        ve.tensor_max(PB[:, 1536:1536 + (b - a)],
                                      PB[:, 1536:1536 + (b - a)],
                                      PB[:, scr_o:scr_o + (b - a)]
                                      ).then_inc(sVE, cVE.add(1))
                    odd_scr_ve[spar] = cVE.v
                ve.wait_ge(sVE, cVE.v)
                if tvexp_sc:
                    ve.wait_ge(sSC, tvexp_sc)
                ve.reduce_max(MQ[:, qt:qt + 1], PMAX[:, qt * 8:qt * 8 + npc],
                              axis=AX.X)
                ve.reduce_max(TV[:, h_o * 2 + qt:h_o * 2 + qt + 1],
                              PB[:, 1536:1536 + w0], axis=AX.X)
                ve.mark(sVE, cVE)
                sc.wait_ge(sVE, cVE.v)
                sc.activation(TV[:, (h_e * 2 + qt):(h_e * 2 + qt) + 1],
                              MQ[:, qt:qt + 1], ACT.Exp, scale=CST[:, 2:3]
                              ).then_inc(sSC, cSC.add(1))
                tvexp_sc = cSC.v
        tvexp_all_sc = tvexp_sc

        if m + 1 < NQB:
            sync.wait_ge(sVE, cVE.v)
            sync.wait_ge(sST, cST.v)
            sync.dma_start(out=x2s_ap, in_=x_t[m + 1]).then_inc(sLX, cLX.add(16))
            ve.wait_ge(sLX, cLX.v)
            ve.wait_ge(sTE, cTE.v)
            ve.wait_ge(sSC, cSC.v)
            ln_pair(LNTM[:, :], LNTMB[:, :], dum=PB, srcb=X2S)
            ln_ready_ve = cVE.v

        psa_half_sc = [0, 0]
        runmax_ve = [cVE.v, cVE.v]
        num_free_sc = [0, 0]
        pend = None
        cmaxred_ve = 0
        mhtrans_te = 0
        te.wait_ge(sVE, cVE.v)

        def emit_pending():
            nonlocal pend, cmaxred_ve, mhtrans_te
            if pend is None:
                return
            hpp, comb_mark = pend
            pend = None
            te.wait_ge(sVE, comb_mark)
            if cmaxred_ve:
                te.wait_ge(sVE, cmaxred_ve)
            for par in range(2):
                for qt in range(2):
                    te.transpose(PSB16[:, par * 256 + qt * 128: par * 256 + (qt + 1) * 128],
                                 MH[:, par * 512 + qt * 128: par * 512 + (qt + 1) * 128],
                                 IDBF[:, :])
            te.mark(sTE, cTE)
            mhtrans_te = cTE.v
            ve.wait_ge(sTE, cTE.v)
            for par in range(2):
                for qt in range(2):
                    col = (2 * hpp + par) * 2 + qt
                    ve.reduce_max(CMAX[:, col:col + 1],
                                  PSB16[:, par * 256 + qt * 128: par * 256 + (qt + 1) * 128],
                                  axis=AX.X)
            ve.mark(sVE, cVE)
            cmaxred_ve = cVE.v

        for hp in range(8):
            ic = hp
            h_e, h_o = 2 * hp, 2 * hp + 1
            rhs_e = QF[0:64, ic * 256: ic * 256 + 256]
            rhs_o = QF[64:128, ic * 256: ic * 256 + 256]
            emit_pending()
            ngr = (nkc + 1) // 2
            qk_mark = {}
            exp_mark = {}
            vm_mark = {}
            pv_mark = {}

            def emit_qk(g):
                kcs = list(range(g * 2, min(g * 2 + 2, nkc)))
                half = g % 2
                if psa_half_sc[half]:
                    te.wait_ge(sSC, psa_half_sc[half])
                for j, kc in enumerate(kcs):
                    te.matmul(PSA[:, half * 1024 + j * 256: half * 1024 + (j + 1) * 256],
                              K_sb[0:64, ic * T + kc * 128: ic * T + (kc + 1) * 128],
                              rhs_e, start=True, stop=True, tile_position=(0, 0))
                    te.matmul(PSA[:, half * 1024 + 512 + j * 256: half * 1024 + 512 + (j + 1) * 256],
                              K_sb[64:128, ic * T + kc * 128: ic * T + (kc + 1) * 128],
                              rhs_o, start=True, stop=True, tile_position=(64, 0))
                te.mark(sTE, cTE)
                qk_mark[g] = cTE.v

            def emit_sv(g):
                kcs = list(range(g * 2, min(g * 2 + 2, nkc)))
                nk = len(kcs)
                assert nk == 2
                half = g % 2
                sc.wait_ge(sTE, qk_mark[g])
                if (g - 2) in pv_mark:
                    sc.wait_ge(sTE, pv_mark[g - 2])
                if runmax_ve[half]:
                    sc.wait_ge(sVE, runmax_ve[half])
                sc.activation(PB[:, half * 1024: half * 1024 + 1024],
                              PSA[:, half * 1024: half * 1024 + 1024],
                              ACT.Exp, scale=CST[:, 2:3]).then_inc(sSC, cSC.add(1))
                exp_mark[g] = cSC.v
                psa_half_sc[half] = cSC.v
                ve.wait_ge(sSC, exp_mark[g])
                if g == m:
                    ve.tensor_mul(PB[:, half * 1024: half * 1024 + 512],
                                  PB[:, half * 1024: half * 1024 + 512],
                                  TRI[:, 0:512])
                    ve.tensor_mul(PB[:, half * 1024 + 512: half * 1024 + 1024],
                                  PB[:, half * 1024 + 512: half * 1024 + 1024],
                                  TRI[:, 0:512])
                    ve.mark(sVE, cVE)
                ve.wait_ge(sVE, cVE.v)
                if g == 0:
                    if mhtrans_te:
                        ve.wait_ge(sTE, mhtrans_te)
                    ve.tensor_copy(MH[:, 0:1024],
                                   PB[:, half * 1024: half * 1024 + 1024])
                else:
                    ve.tensor_max(MH[:, 0:1024], MH[:, 0:1024],
                                  PB[:, half * 1024: half * 1024 + 1024])
                ve.mark(sVE, cVE)
                vm_mark[g] = cVE.v
                runmax_ve[half] = cVE.v

            def emit_pv(g):
                kcs = list(range(g * 2, min(g * 2 + 2, nkc)))
                half = g % 2
                te.wait_ge(sVE, vm_mark[g])
                if g == 0:
                    if num_free_sc[0]:
                        te.wait_ge(sSC, num_free_sc[0])
                    if num_free_sc[1]:
                        te.wait_ge(sSC, num_free_sc[1])
                for j, kc in enumerate(kcs):
                    te.matmul(PSN[0:65, 0:256],
                              V_sb[:, kc * (H * 65) + h_e * 65: kc * (H * 65) + h_e * 65 + 65],
                              PB[:, half * 1024 + j * 256: half * 1024 + (j + 1) * 256],
                              start=(kc == 0), stop=(kc == nkc - 1))
                    te.matmul(PSC[0:65, 0:256],
                              V_sb[:, kc * (H * 65) + h_o * 65: kc * (H * 65) + h_o * 65 + 65],
                              PB[:, half * 1024 + 512 + j * 256: half * 1024 + 512 + (j + 1) * 256],
                              start=(kc == 0), stop=(kc == nkc - 1))
                te.mark(sTE, cTE)
                pv_mark[g] = cTE.v
                return cTE.v

            emit_qk(0)
            if ngr > 1:
                emit_qk(1)
            emit_sv(0)
            if ngr > 2:
                emit_qk(2)
            if ngr > 1:
                emit_sv(1)
            pv_last = None
            for g in range(ngr):
                pv_last = emit_pv(g)
                if g + 3 < ngr:
                    emit_qk(g + 3)
                if g + 2 < ngr:
                    emit_sv(g + 2)
            sc.wait_ge(sTE, pv_last)
            sc.copy(NUMBUF[0:65, h_e * 256:(h_e + 1) * 256], PSN[0:65, 0:256])
            sc.copy(NUMBUF[0:65, h_o * 256:(h_o + 1) * 256],
                    PSC[0:65, 0:256]).then_inc(sSC, cSC.add(1))
            num_free_sc[0] = cSC.v
            num_free_sc[1] = cSC.v
            ve.wait_ge(sVE, cVE.v)
            ve.tensor_max(MH[:, 0:256], MH[:, 0:256], MH[:, 256:512])
            ve.tensor_max(MH[:, 512:768], MH[:, 512:768],
                          MH[:, 768:1024]).then_inc(sVE, cVE.add(1))
            pend = (hp, cVE.v)
        emit_pending()

        if m + 1 < NQB:
            te.wait_ge(sVE, cVE.v)
            for sub, src in ((0, LNTM[:, :]), (1, LNTMB[:, :])):
                te.wait_ge(sSC, cSC.v)
                for c in range(8):
                    te.transpose(PSB16[:, c * 128:(c + 1) * 128],
                                 src[:, c * 128:(c + 1) * 128], IDBF[:, :])
                te.mark(sTE, cTE)
                sc.wait_ge(sTE, cTE.v)
                dst = LNFM.ap().rearrange("p (dc t) -> p dc t", dc=8)[
                    :, :, sub * 128:(sub + 1) * 128]
                sc.copy(dst, PSB16.ap().rearrange("p (dc t) -> p dc t", dc=8)
                        ).then_inc(sSC, cSC.add(1))

        ve.wait_ge(sSC, tvexp_all_sc)
        ve.wait_ge(sVE, cVE.v)
        ve.tensor_max(TV[:, 0:32], TV[:, 0:32], CMAX[:, 0:32])
        ve.mark(sVE, cVE)
        te.wait_ge(sSC, cSC.v)
        te.wait_ge(sVE, cVE.v)
        for h in range(H):
            for qt in range(2):
                te.transpose(PSB16[:, 512 + (h * 2 + qt) * 2: 513 + (h * 2 + qt) * 2],
                             NUMBUF[64:65, h * 256 + qt * 128: h * 256 + (qt + 1) * 128],
                             IDBF[64:65, 64:65])
        te.mark(sTE, cTE)
        ve.wait_ge(sTE, cTE.v)
        den_ap = PSB16.ap().rearrange("p (c two) -> p c two", two=2)[:, 256:288, 0:1]
        tv_ap = TV.ap().rearrange("p (c one) -> p c one", one=1)[:, 0:32, 0:1]
        scr_ap = SCR.ap().rearrange("p (c one) -> p c one", one=1)[:, 48:80, 0:1]
        ve.scalar_tensor_tensor(scr_ap, tv_ap, EPS_ATTN,
                                den_ap, ALU.mult, ALU.add)
        ve.mark(sVE, cVE)
        ve.wait_ge(sVE, cVE.v)
        ve.reciprocal(SCR[:, 80:112], SCR[:, 48:80])
        ve.mark(sVE, cVE)
        recip_ve = cVE.v
        sc.wait_ge(sST, cST.v)
        sc.wait_ge(sVE, ln_ready_ve)
        ve.wait_ge(sST, cST.v)
        te.wait_ge(sVE, recip_ve)
        amult_ve = {}
        evac_sc = {}
        bc_mark = {}
        for rnd in range(2):
            if rnd == 1:
                te.wait_ge(sSC, cSC.v)
            for hl in range(8):
                h = rnd * 8 + hl
                for qt in range(2):
                    te.transpose(PSA[0:1, hl * 256 + qt * 128: hl * 256 + (qt + 1) * 128],
                                 SCR[:, 80 + h * 2 + qt:81 + h * 2 + qt], IDF[:, :])
            te.mark(sTE, cTE)
            rt_mark = cTE.v
            for hl in range(8):
                h = rnd * 8 + hl
                par = h % 2
                numps = PSN if par == 0 else PSC
                slot = (h // 2) % 2
                sslot = h % 2
                sc.wait_ge(sTE, rt_mark)
                if h >= 2:
                    sc.wait_ge(sTE, cTE.v)
                sc.copy(X2S[0:1, 1024 + sslot * 256: 1024 + (sslot + 1) * 256],
                        PSA[0:1, hl * 256:(hl + 1) * 256]).then_inc(sSC, cSC.add(1))
                evac_sc[h] = cSC.v
                te.wait_ge(sSC, evac_sc[h])
                if h >= 4 and (h - 4) in amult_ve:
                    te.wait_ge(sVE, amult_ve[h - 4])
                te.matmul(numps[0:64, slot * 256:(slot + 1) * 256],
                          ONESF[0:1, 0:64],
                          X2S[0:1, 1024 + sslot * 256: 1024 + (sslot + 1) * 256],
                          start=True, stop=True)
                te.mark(sTE, cTE)
                bc_mark[h] = cTE.v
                ve.wait_ge(sTE, cTE.v)
                if par == 0:
                    dst = ATTN[0:64, (h // 2) * 256:(h // 2 + 1) * 256]
                else:
                    dst = ATTS[0:64, (h // 2) * 256:(h // 2 + 1) * 256]
                ve.scalar_tensor_tensor(dst, NUMBUF[0:64, h * 256:(h + 1) * 256], 1.0,
                                        numps[0:64, slot * 256:(slot + 1) * 256],
                                        ALU.mult, ALU.mult).then_inc(sVE, cVE.add(1))
                amult_ve[h] = cVE.v
                if par == 1:
                    sync.wait_ge(sVE, amult_ve[h])
                    sync.dma_start(out=ATTN[64:128, (h // 2) * 256:(h // 2 + 1) * 256],
                                   in_=ATTS[0:64, (h // 2) * 256:(h // 2 + 1) * 256]
                                   ).then_inc(sSH, cSH.add(16))

        if m + 1 < NQB:
            te.wait_ge(sSC, cSC.v)
            te.wait_ge(sGD, cGD.v)
            pmarks = proj_1024(lambda ic: QF[:, ic * 256:(ic + 1) * 256])
        else:
            pmarks = [wb_free_te] * 8
        wo_src = wo_d.rearrange("(dc p) i -> p dc i", p=128)
        wb_dst = WB.ap().rearrange("p (dc i) -> p dc i", dc=8)
        for icw in range(8):
            gp.wait_ge(sTE, pmarks[icw])
            gp.dma_start(out=wb_dst[:, :, icw * 128:(icw + 1) * 128],
                         in_=wo_src[:, :, icw * 128:(icw + 1) * 128]
                         ).then_inc(sGD, cGD.add(16))

        te.wait_ge(sSH, cSH.v)
        te.wait_ge(sVE, cVE.v)
        te.wait_ge(sGD, cGD.v)
        te.wait_ge(sSC, cSC.v)
        for sub in range(2):
            for dh2 in range(2):
                for icc in range(8):
                    te.matmul(PSA[:, sub * 1024 + dh2 * 512: sub * 1024 + (dh2 + 1) * 512],
                              ATTN[:, icc * 256 + sub * 128: icc * 256 + (sub + 1) * 128],
                              WB[:, icc * D + dh2 * 512: icc * D + (dh2 + 1) * 512],
                              start=(icc == 0), stop=(icc == 7))
        te.mark(sTE, cTE)
        wb_free_te = cTE.v
        ve.wait_ge(sTE, cTE.v)
        ve.wait_ge(sLD, cLD.v)
        ve.wait_ge(sST, cST.v)
        ve.tensor_add(XTc[:, 0:D], XTc[:, 0:D], BOB[:, :])
        ve.tensor_add(XTc[:, D:2 * D], XTc[:, D:2 * D], BOB[:, :]).then_inc(sVE, cVE.add(1))
        ve.wait_ge(sVE, cVE.v)
        for sub in range(2):
            for dh2 in range(2):
                ve.scalar_tensor_tensor(
                    X2S[:, sub * 1024 + dh2 * 512: sub * 1024 + (dh2 + 1) * 512],
                    PSA[:, sub * 1024 + dh2 * 512: sub * 1024 + (dh2 + 1) * 512],
                    1.0, XTc[:, sub * D + dh2 * 512: sub * D + (dh2 + 1) * 512],
                    ALU.mult, ALU.add)
        ve.mark(sVE, cVE)
        xt_free_ve = cVE.v
        sync.wait_ge(sVE, cVE.v)
        sync.dma_start(out=x2_t[m], in_=x2s_ap).then_inc(sST, cST.add(16))
        if m + 2 < NQB:
            gp.wait_ge(sTE, cTE.v)
            gp.dma_start(out=WB.ap().rearrange("p (dc i) -> p dc i", dc=8),
                         in_=wq_d.rearrange("(dc p) i -> p dc i", p=128)
                         ).then_inc(sGD, cGD.add(16))

    nc.all_engine_barrier()

    wet_r = wet_d.rearrange("e (hc p) f -> p e hc f", p=128)
    wt_r = WT.ap().rearrange("p (e hc f) -> p e hc f", e=E, hc=8)
    for e in range(E):
        gp.dma_start(out=wt_r[:, e], in_=wet_r[:, e]).then_inc(sGD, cGD.add(16))
    a2_store = cST.v
    bsets = [(XT, LN2TM, LN2F32, MBC, sLD, cLD),
             (XTB2, LN2TMb, LN2F32, MBC, sLX, cLX)]
    hdr_ve = {}
    hdr_te = {}
    mask_ve = {}
    fin_ve = {}

    def hdrA(tt):
        XTb, TMb, F32b, MBCb_, sem, ct = bsets[tt % 2]
        if tt >= 2:
            sync.wait_ge(sVE, fin_ve.get(tt - 2, 0))
        sync.wait_ge(sST, a2_store)
        sync.dma_start(out=XTb.ap().rearrange("p (s c) -> p s c", s=2),
                       in_=x2_t[tt]).then_inc(sem, ct.add(16))
        ve.wait_ge(sem, ct.v)
        if tt >= 2:
            ve.wait_ge(sVE, cVE.v)
        ln_pair(TMb[:, 0:D], TMb[:, D:2 * D], dum=DUMB, srcb=XTb)
        hdr_ve[tt] = cVE.v

    def hdrB(tt):
        XTb, TMb, F32b, MBCb_, sem, ct = bsets[tt % 2]
        te.wait_ge(sVE, hdr_ve[tt])
        for sub in range(2):
            src = TMb[:, sub * D:(sub + 1) * D]
            for rnd in range(2):
                te.wait_ge(sSC, cSC.v)
                for c in range(4):
                    dc = rnd * 4 + c
                    te.transpose(PSB[:, c * 128:(c + 1) * 128],
                                 src[:, dc * 128:(dc + 1) * 128], IDF[:, :])
                te.mark(sTE, cTE)
                sc.wait_ge(sTE, cTE.v)
                dst = F32b.ap().rearrange("p (dc t) -> p dc t", dc=8)[
                    :, rnd * 4:(rnd + 1) * 4, sub * 128:(sub + 1) * 128]
                sc.copy(dst, PSB.ap().rearrange("p (c t) -> p c t", c=4)
                        ).then_inc(sSC, cSC.add(1))
        te.wait_ge(sSC, cSC.v)
        for sub in range(2):
            for hc in range(8):
                te.matmul(PSB[:, sub * 8:(sub + 1) * 8],
                          F32b[:, hc * 256 + sub * 128: hc * 256 + (sub + 1) * 128],
                          GWT[:, hc * 8:(hc + 1) * 8],
                          start=(hc == 0), stop=(hc == 7))
        te.mark(sTE, cTE)
        ve.wait_ge(sTE, cTE.v)
        if tt >= 1:
            ve.wait_ge(sVE, cVE.v)
        for sub in range(2):
            ve.reduce_max(MQ[:, sub:sub + 1], PSB[:, sub * 8:(sub + 1) * 8], axis=AX.X)
        ve.mark(sVE, cVE)
        ve.wait_ge(sVE, cVE.v)
        for sub in range(2):
            ve.tensor_scalar(EQ8[:, sub * 8:(sub + 1) * 8], PSB[:, sub * 8:(sub + 1) * 8],
                             MQ[:, sub:sub + 1], None, ALU.is_equal)
        ve.mark(sVE, cVE)
        ve.wait_ge(sVE, cVE.v)
        for sub in range(2):
            ve.scalar_tensor_tensor(SCR[:, 32 + sub * 8: 40 + sub * 8],
                                    EQ8[:, sub * 8:(sub + 1) * 8], -16.0,
                                    IOTA[:, 8:16], ALU.mult, ALU.add)
        ve.mark(sVE, cVE)
        ve.wait_ge(sVE, cVE.v)
        for sub in range(2):
            ve.tensor_reduce(IDX[:, sub:sub + 1], SCR[:, 32 + sub * 8: 40 + sub * 8],
                             AX.X, ALU.min)
        ve.mark(sVE, cVE)
        te.wait_ge(sVE, cVE.v)
        for sub in range(2):
            te.transpose(PSB[0:1, 16 + sub * 128: 16 + (sub + 1) * 128],
                         IDX[:, sub:sub + 1], IDF[:, :])
        te.mark(sTE, cTE)
        sc.wait_ge(sTE, cTE.v)
        sc.wait_ge(sST, cST.v)
        sc.copy(X2S[0:1, 1024:1280], PSB[0:1, 16:272]).then_inc(sSC, cSC.add(1))
        te.wait_ge(sSC, cSC.v)
        if tt >= 1:
            te.wait_ge(sVE, mask_ve.get(tt - 1, 0))
        te.matmul(PSC[:, 0:256], ONESF[0:1, 0:128], X2S[0:1, 1024:1280],
                  start=True, stop=True)
        te.mark(sTE, cTE)
        ve.wait_ge(sTE, cTE.v)
        ve.wait_ge(sVE, cVE.v)
        for e in range(E):
            ve.tensor_scalar(MBCb_[:, e * 256:(e + 1) * 256], PSC[:, 0:256],
                             float(e), None, ALU.is_equal)
        ve.mark(sVE, cVE)
        mask_ve[tt] = cVE.v
        hdr_te[tt] = cTE.v

    hdrA(0)
    hdrB(0)
    for tt in range(NQB):
        XTb, TMb, F32b, MBCb_, _, _ = bsets[tt % 2]
        if tt + 1 < NQB:
            hdrA(tt + 1)
        if tt == 0:
            te.wait_ge(sGD, cGD.v)
        NEH = E * 8
        mln_ve = {}
        mm_te = {}

        def emit_mln(i):
            e, hc = i // 8, i % 8
            buf = i % 2
            if (i - 2) in mm_te:
                ve.wait_ge(sTE, mm_te[i - 2])
            ve.tensor_mul(MLN[:, buf * 256:(buf + 1) * 256],
                          F32b[:, hc * 256:(hc + 1) * 256],
                          MBCb_[:, e * 256:(e + 1) * 256]).then_inc(sVE, cVE.add(1))
            mln_ve[i] = cVE.v

        ve.wait_ge(sVE, mask_ve[tt])
        ve.wait_ge(sTE, cTE.v)
        emit_mln(0)
        for i in range(NEH):
            if i + 1 < NEH:
                emit_mln(i + 1)
            buf = i % 2
            te.wait_ge(sVE, mln_ve[i])
            if i == 0:
                te.wait_ge(sVE, fin_ve.get(tt - 1, 0))
            for sub in range(2):
                for fh in range(2):
                    te.matmul(PSA[:, sub * 1024 + fh * 512: sub * 1024 + (fh + 1) * 512],
                              MLN[:, buf * 256 + sub * 128: buf * 256 + (sub + 1) * 128],
                              WT[:, i * D + fh * 512: i * D + (fh + 1) * 512],
                              start=(i == 0), stop=(i == NEH - 1))
            te.mark(sTE, cTE)
            mm_te[i] = cTE.v
        ve.wait_ge(sTE, cTE.v)
        ve.wait_ge(sST, cST.v)
        for sub in range(2):
            for fh in range(2):
                ve.scalar_tensor_tensor(
                    X2S[:, sub * 1024 + fh * 512: sub * 1024 + (fh + 1) * 512],
                    PSA[:, sub * 1024 + fh * 512: sub * 1024 + (fh + 1) * 512],
                    1.0, TMb[:, sub * D + fh * 512: sub * D + (fh + 1) * 512],
                    ALU.mult, ALU.add)
        ve.mark(sVE, cVE)
        ve.wait_ge(sVE, cVE.v)
        for sub in range(2):
            for fh in range(2):
                ve.tensor_add(
                    X2S[:, sub * 1024 + fh * 512: sub * 1024 + (fh + 1) * 512],
                    X2S[:, sub * 1024 + fh * 512: sub * 1024 + (fh + 1) * 512],
                    XTb[:, sub * D + fh * 512: sub * D + (fh + 1) * 512])
        ve.mark(sVE, cVE)
        fin_ve[tt] = cVE.v
        sync.wait_ge(sVE, cVE.v)
        sync.dma_start(out=out_t[tt], in_=x2s_ap).then_inc(sST, cST.add(16))
        if tt + 1 < NQB:
            hdrB(tt + 1)

    sync.wait_ge(sST, cST.v)
    if dbg is not None:
        dbg.update(locals())
    return nc


def _consts():
    ident_bf = np.eye(128, dtype=ml_dtypes.bfloat16)
    ident_f = np.eye(128, dtype=np.float32)
    k = np.arange(128)[:, None]
    q = np.arange(128)[None, :]
    tri = (k <= q).astype(np.float32)
    tri0 = np.concatenate([tri, np.ones((128, 128), np.float32)], axis=1)
    tri1 = np.concatenate([np.zeros((128, 128), np.float32), tri], axis=1)
    tri_all = np.concatenate([tri0, tri1], axis=1).astype(ml_dtypes.bfloat16)
    iota = np.broadcast_to(np.arange(8, dtype=np.float32), (128, 8))
    iota16 = np.concatenate([iota, iota + 16.0], axis=1).astype(np.float32)
    ones_f = np.ones((1, 128), dtype=np.float32)
    cst = np.zeros((128, 8), dtype=np.float32)
    cst[:, 0] = EPS_LN
    cst[:, 1] = -0.5
    cst[:, 2] = SCALE
    return ident_bf, ident_f, tri_all, np.ascontiguousarray(iota16), ones_f, cst


def kernel(L0_attn_x, L0_attn_Wq, L0_attn_Wkv, L0_attn_Wo, L0_attn_bo,
           L0_ffn_gate_w, L0_ffn_expert_w):
    x = np.ascontiguousarray(np.asarray(L0_attn_x, dtype=np.float32))
    wq = np.ascontiguousarray(np.asarray(L0_attn_Wq, dtype=np.float32))
    wkv = np.ascontiguousarray(np.asarray(L0_attn_Wkv, dtype=np.float32))
    wo = np.ascontiguousarray(np.asarray(L0_attn_Wo, dtype=np.float32))
    bo = np.asarray(L0_attn_bo, dtype=np.float32)
    gw = np.asarray(L0_ffn_gate_w, dtype=np.float32)
    we = np.asarray(L0_ffn_expert_w, dtype=np.float32)
    B, T, _ = x.shape
    bo_b = np.ascontiguousarray(np.broadcast_to(bo[None, :], (128, D)))
    gwt = np.ascontiguousarray(gw.T)
    wet = np.ascontiguousarray(we.transpose(0, 2, 1))
    ident_bf, ident_f, tri_all, iota16, ones_f, cst = _consts()

    nc = build(T=T)
    shared = {
        "wq": wq, "wkv": wkv, "wo": wo, "bo_b": bo_b, "gwt": gwt, "wet": wet,
        "id_bf": ident_bf, "id_f32": ident_f, "tri": tri_all,
        "iota8": iota16, "ones_f": ones_f, "cst": cst,
    }
    in_maps = [dict(shared, x=np.ascontiguousarray(x[b])) for b in range(B)]
    res = run_bass_kernel_spmd(nc, in_maps, core_ids=list(range(B)))
    out = np.stack([np.asarray(r["out"]) for r in res.results], axis=0)
    return out.astype(np.float32)


# version 19
# speedup vs baseline: 1.0973x; 1.0434x over previous
import numpy as np
import ml_dtypes
import concourse.bass as bass
import concourse.mybir as mybir
from concourse.bass_utils import run_bass_kernel_spmd

F32 = mybir.dt.float32
BF16 = mybir.dt.bfloat16
AX = mybir.AxisListType
ALU = mybir.AluOpType
ACT = mybir.ActivationFunctionType

D = 1024
H = 16
DH = 64
E = 8
SCALE = DH ** -0.5
EPS_ATTN = 1e-8
EPS_LN = 1e-5



class EngW:

    def __init__(self, eng):
        self._e = eng
        self.last = None

    def __getattr__(self, name):
        f = getattr(self._e, name)
        if not callable(f):
            return f
        rec = name not in ("wait_ge", "nop")

        def w(*a, **k):
            r = f(*a, **k)
            if rec and hasattr(r, "then_inc"):
                self.last = r
            return r
        return w

    def mark(self, sem, ct):
        self.last.then_inc(sem, ct.add(1))


class Ct:
    def __init__(self):
        self.v = 0

    def add(self, n):
        self.v += n
        return n


def build(T=4096, dbg=None):
    QB = 256
    NQB = T // QB
    NKC = T // 128

    nc = bass.Bass("TRN2", target_bir_lowering=False)

    x_d = nc.dram_tensor("x", [T, D], F32, kind="ExternalInput").ap()
    wq_d = nc.dram_tensor("wq", [D, D], F32, kind="ExternalInput").ap()
    wkv_d = nc.dram_tensor("wkv", [D, 2 * D], F32, kind="ExternalInput").ap()
    wo_d = nc.dram_tensor("wo", [D, D], F32, kind="ExternalInput").ap()
    bo_d = nc.dram_tensor("bo_b", [128, D], F32, kind="ExternalInput").ap()
    gwt_d = nc.dram_tensor("gwt", [D, E], F32, kind="ExternalInput").ap()
    wet_d = nc.dram_tensor("wet", [E, D, D], F32, kind="ExternalInput").ap()
    idbf_d = nc.dram_tensor("id_bf", [128, 128], BF16, kind="ExternalInput").ap()
    idf32_d = nc.dram_tensor("id_f32", [128, 128], F32, kind="ExternalInput").ap()
    tri_d = nc.dram_tensor("tri", [128, 512], BF16, kind="ExternalInput").ap()
    iota_d = nc.dram_tensor("iota8", [128, 16], F32, kind="ExternalInput").ap()
    ones_d = nc.dram_tensor("ones_f", [1, 128], F32, kind="ExternalInput").ap()
    cst_d = nc.dram_tensor("cst", [128, 8], F32, kind="ExternalInput").ap()
    out_d = nc.dram_tensor("out", [T, D], F32, kind="ExternalOutput").ap()
    x2_d = nc.dram_tensor("x2scr", [T, D], F32, kind="Internal").ap()

    sLD = nc.alloc_semaphore("sLD")
    sLX = nc.alloc_semaphore("sLX")
    sLC = nc.alloc_semaphore("sLC")
    sGD = nc.alloc_semaphore("sGD")
    sST = nc.alloc_semaphore("sST")
    sSH = nc.alloc_semaphore("sSH")
    sTE = nc.alloc_semaphore("sTE")
    sSC = nc.alloc_semaphore("sSC")
    sVE = nc.alloc_semaphore("sVE")
    sGP = nc.alloc_semaphore("sGP")
    cLD, cLX, cLC, cGD, cST, cSH, cTE, cSC, cVE, cGP = (Ct() for _ in range(10))

    ARENA_BYTES = 210800
    _arena = nc.alloc_sbuf_tensor("ARENA", [128, ARENA_BYTES], mybir.dt.uint8)
    _base = nc.lookup_mloc(_arena).addr
    _off = [_base]

    def _at(name, shape, dtype, region_end=None):
        nbytes = shape[1] * (2 if dtype == BF16 else 4)
        addr = (_off[0] + 31) & ~31
        h = nc.alloc_sbuf_tensor_at(name, shape, dtype, offset=addr)
        _off[0] = addr + nbytes
        return h

    XT = _at("XT", [128, 2 * D], F32)
    _x2s_off = (_off[0] + 31) & ~31
    X2S = _at("X2S", [128, 2048], F32)
    SCR = _at("SCR", [128, 128], F32)
    BOB = _at("BOB", [128, D], BF16)
    IDBF = _at("IDBF", [128, 128], BF16)
    IDF = _at("IDF", [128, 128], F32)
    TRI = _at("TRI", [128, 512], BF16)
    IOTA = _at("IOTA", [128, 16], F32)
    ONESF = _at("ONESF", [1, 128], F32)
    CST = _at("CST", [128, 8], F32)
    GWT = _at("GWT", [128, 64], F32)
    MQ = _at("MQ", [128, 4], F32)
    EQ8 = _at("EQ8", [128, 16], F32)
    IDX = _at("IDX", [128, 2], F32)
    TV = _at("TV", [128, 32], F32)
    PMAX = _at("PMAX", [128, 16], F32)
    MH = _at("MH", [128, 1024], BF16)
    CMAX = _at("CMAX", [128, 32], F32)
    LNTM = _at("LNTM", [128, D], BF16)
    LNTMB = _at("LNTMB", [128, D], BF16)
    ATTS = nc.alloc_sbuf_tensor_at("ATTS", [128, 2048], BF16, offset=_x2s_off)
    MLN = _at("MLN", [128, 512], BF16)
    shared_end = _off[0]
    _k_off = (_off[0] + 31) & ~31
    WB = _at("WB", [128, 8 * D], BF16)
    K_sb = _at("K", [128, 8 * T], BF16)
    V_sb = _at("V", [128, NKC * (H * 65)], BF16)
    QF = _at("QF", [128, 2048], BF16)
    PB = _at("PB", [128, 2048], BF16)
    ATTN = _at("ATTN", [128, 2048], BF16)
    LNFM = _at("LNFM", [128, 2048], BF16)
    NUMBUF = _at("NUMBUF", [128, 4096], BF16)
    LNFM2 = _at("LNFM2", [128, 2048], BF16)
    LNTM2 = _at("LNTM2", [128, D], BF16)
    LNTMB2 = _at("LNTMB2", [128, D], BF16)
    a_end = _off[0]
    _off[0] = shared_end
    WT = _at("WT", [128, 64 * D], BF16)
    LN2TM = _at("LN2TM", [128, 2048], F32)
    LN2F32 = _at("LN2F32", [128, 2048], F32)
    MBC = _at("MBC", [128, 2048], BF16)
    XTB2 = _at("XTB2", [128, 2048], F32)
    LN2TMb = _at("LN2TMb", [128, 2048], F32)
    DUMB = _at("DUMB", [128, 2048], BF16)
    b_end = _off[0]
    assert max(a_end, b_end) < _base + ARENA_BYTES, (_base, shared_end, a_end, b_end)
    LN2F16 = LN2F32
    PSA = nc.alloc_psum_tensor("PSA", [128, 2048], F32)
    PSB = nc.alloc_psum_tensor("PSB", [128, 512], F32)
    PSB16 = nc.alloc_psum_tensor("PSB16", [128, 1024], BF16)
    PSN = nc.alloc_psum_tensor("PSN", [128, 512], F32)
    PSC = nc.alloc_psum_tensor("PSC", [128, 512], F32)

    sync, gp, te, sc, ve = (EngW(nc.sync), EngW(nc.gpsimd), EngW(nc.tensor),
                            EngW(nc.scalar), EngW(nc.vector))

    sync.dma_start(out=IDBF[:, :], in_=idbf_d).then_inc(sLC, cLC.add(16))
    sync.dma_start(out=IDF[:, :], in_=idf32_d).then_inc(sLC, cLC.add(16))
    sync.dma_start(out=TRI[:, :], in_=tri_d).then_inc(sLC, cLC.add(16))
    sync.dma_start(out=IOTA[:, :], in_=iota_d).then_inc(sLC, cLC.add(16))
    sync.dma_start(out=ONESF[:, :], in_=ones_d).then_inc(sLC, cLC.add(16))
    sync.dma_start(out=CST[:, :], in_=cst_d).then_inc(sLC, cLC.add(16))
    gp.dma_start(out=BOB[:, :], in_=bo_d).then_inc(sGD, cGD.add(16))
    sync.dma_start(out=GWT.ap().rearrange("p (hc e) -> p hc e", hc=8),
                   in_=gwt_d.rearrange("(hc p) e -> p hc e", p=128)).then_inc(sLC, cLC.add(16))
    gp.dma_start(out=WB.ap().rearrange("p (dc i) -> p dc i", dc=8),
                 in_=wkv_d.rearrange("(dc p) i -> p dc i", p=128)[:, :, 0:D]
                 ).then_inc(sGD, cGD.add(16))
    ve.wait_ge(sLC, cLC.v)
    sc.wait_ge(sLC, cLC.v)
    te.wait_ge(sLC, cLC.v)
    ve.memset(V_sb.ap().rearrange("p (k h j) -> p k h j", k=NKC, h=H)[:, :, :, 64:65], 1.0)
    ve.mark(sVE, cVE)

    x_t = x_d.rearrange("(n s p) c -> n p s c", s=2, p=128)
    x2_t = x2_d.rearrange("(n s p) c -> n p s c", s=2, p=128)
    out_t = out_d.rearrange("(n s p) c -> n p s c", s=2, p=128)
    xt_ap = XT.ap().rearrange("p (s c) -> p s c", s=2)
    x2s_ap = X2S.ap().rearrange("p (s c) -> p s c", s=2)

    def ln_pair(out0, out1, dum=None, srcb=None):
        cols = [SCR[:, c:c + 1] for c in range(16)]
        (s_sum0, s_ss0, s_m0, s_e0, s_v0, s_l0, s_r0, _,
         s_sum1, s_ss1, s_m1, s_e1, s_v1, s_l1, s_r1, _) = cols
        if srcb is None:
            srcb = XT
        src0 = srcb[:, 0:D]
        src1 = srcb[:, D:2 * D]
        ve.reduce_sum(s_sum0, src0, axis=AX.X)
        ve.reduce_sum(s_sum1, src1, axis=AX.X)
        ve.scalar_tensor_tensor(dum[:, 0:D], src0, 1.0, src0, ALU.mult, ALU.mult,
                                accum_out=s_ss0)
        ve.scalar_tensor_tensor(dum[:, D:2 * D], src1, 1.0, src1, ALU.mult, ALU.mult,
                                accum_out=s_ss1).then_inc(sVE, cVE.add(1))
        ve.wait_ge(sVE, cVE.v)
        ve.tensor_scalar_mul(s_m0, s_sum0, 1.0 / D)
        ve.tensor_scalar_mul(s_m1, s_sum1, 1.0 / D)
        ve.tensor_scalar_mul(s_e0, s_ss0, 1.0 / D)
        ve.tensor_scalar_mul(s_e1, s_ss1, 1.0 / D).then_inc(sVE, cVE.add(1))
        ve.wait_ge(sVE, cVE.v)
        ve.tensor_mul(s_v0, s_m0, s_m0)
        ve.tensor_mul(s_v1, s_m1, s_m1).then_inc(sVE, cVE.add(1))
        ve.wait_ge(sVE, cVE.v)
        ve.tensor_sub(s_v0, s_e0, s_v0)
        ve.tensor_sub(s_v1, s_e1, s_v1).then_inc(sVE, cVE.add(1))
        sc.wait_ge(sVE, cVE.v)
        sc.activation(s_l0, s_v0, ACT.Ln, bias=CST[:, 0:1])
        sc.activation(s_l1, s_v1, ACT.Ln, bias=CST[:, 0:1]).then_inc(sSC, cSC.add(1))
        sc.wait_ge(sSC, cSC.v)
        sc.activation(s_r0, s_l0, ACT.Exp, scale=CST[:, 1:2])
        sc.activation(s_r1, s_l1, ACT.Exp, scale=CST[:, 1:2]).then_inc(sSC, cSC.add(1))
        ve.wait_ge(sSC, cSC.v)
        ve.tensor_scalar(out0, src0, s_m0[:, 0:1], s_r0[:, 0:1], ALU.subtract, ALU.mult)
        ve.tensor_scalar(out1, src1, s_m1[:, 0:1], s_r1[:, 0:1],
                         ALU.subtract, ALU.mult).then_inc(sVE, cVE.add(1))


    def emit_ln_to_fm(dst_fm, f32=False, dst32=None, srcb=None, tm0=None, tm1=None):
        if f32:
            ln_pair(dst32[:, 0:D], dst32[:, D:2 * D], dum=DUMB, srcb=srcb)
            src0, src1 = dst32[:, 0:D], dst32[:, D:2 * D]
        else:
            t0_, t1_ = (LNTM, LNTMB) if tm0 is None else (tm0, tm1)
            ln_pair(t0_[:, :], t1_[:, :], dum=PB, srcb=srcb)
            src0, src1 = t0_[:, :], t1_[:, :]
        te.wait_ge(sVE, cVE.v)
        for sub, src in ((0, src0), (1, src1)):
            if f32:
                for rnd in range(2):
                    te.wait_ge(sSC, cSC.v)
                    for c in range(4):
                        dc = rnd * 4 + c
                        te.transpose(PSB[:, c * 128:(c + 1) * 128],
                                     src[:, dc * 128:(dc + 1) * 128], IDF[:, :])
                    te.mark(sTE, cTE)
                    sc.wait_ge(sTE, cTE.v)
                    dst = dst_fm.ap().rearrange("p (dc t) -> p dc t", dc=8)[
                        :, rnd * 4:(rnd + 1) * 4, sub * 128:(sub + 1) * 128]
                    sc.copy(dst, PSB.ap().rearrange("p (c t) -> p c t", c=4)
                            ).then_inc(sSC, cSC.add(1))
            else:
                te.wait_ge(sSC, cSC.v)
                for c in range(8):
                    te.transpose(PSB16[:, c * 128:(c + 1) * 128],
                                 src[:, c * 128:(c + 1) * 128], IDBF[:, :])
                te.mark(sTE, cTE)
                sc.wait_ge(sTE, cTE.v)
                dst = dst_fm.ap().rearrange("p (dc t) -> p dc t", dc=8)[
                    :, :, sub * 128:(sub + 1) * 128]
                sc.copy(dst, PSB16.ap().rearrange("p (dc t) -> p dc t", dc=8)
                        ).then_inc(sSC, cSC.add(1))
        te.wait_ge(sSC, cSC.v)

    def proj_1024(dst_sb, rhs_cols=256, fm=None):
        if fm is None:
            fm = LNFM
        marks = []
        for ic in range(8):
            bank = ic % 4
            if ic >= 4:
                te.wait_ge(sSC, cSC.v)
            for dc in range(8):
                te.matmul(PSA[:, bank * 512:bank * 512 + 256],
                          WB[:, dc * D + ic * 128:dc * D + (ic + 1) * 128],
                          fm[:, dc * 256:(dc + 1) * 256],
                          start=(dc == 0), stop=(dc == 7))
            te.mark(sTE, cTE)
            marks.append(cTE.v)
            sc.wait_ge(sTE, cTE.v)
            sc.copy(dst_sb(ic), PSA[:, bank * 512:bank * 512 + 256]
                    ).then_inc(sSC, cSC.add(1))
        return marks

    p1_xb = [(XT, xt_ap, sLD, cLD), (X2S, x2s_ap, sLX, cLX)]
    p1_fm = [(LNFM, LNTM, LNTMB), (LNFM2, LNTM2, LNTMB2)]
    p1_xfree = [0, 0]
    sync.dma_start(out=xt_ap, in_=x_t[0]).then_inc(sLD, cLD.add(16))
    for pas in (0, 1):
        if pas == 1:
            gp.wait_ge(sTE, cTE.v)
            gp.dma_start(out=WB.ap().rearrange("p (dc i) -> p dc i", dc=8),
                         in_=wkv_d.rearrange("(dc p) i -> p dc i", p=128)[:, :, D:2 * D]
                         ).then_inc(sGD, cGD.add(16))
        for tt in range(NQB):
            g = pas * NQB + tt
            xb, xap, sem, ct = p1_xb[g % 2]
            fm, tm0, tm1 = p1_fm[g % 2]
            if g + 1 < 2 * NQB:
                xb2, xap2, sem2, ct2 = p1_xb[(g + 1) % 2]
                if p1_xfree[(g + 1) % 2]:
                    sync.wait_ge(sVE, p1_xfree[(g + 1) % 2])
                sync.dma_start(out=xap2, in_=x_t[(g + 1) % NQB]
                               ).then_inc(sem2, ct2.add(16))
            ve.wait_ge(sem, ct.v)
            if pas == 0 and tt == 0:
                ve.wait_ge(sGD, cGD.v)
            emit_ln_to_fm(fm, srcb=xb, tm0=tm0, tm1=tm1)
            p1_xfree[g % 2] = cVE.v
            te.wait_ge(sGD, cGD.v)
            if pas == 0:
                proj_1024(lambda ic, _tt=tt: K_sb[:, ic * T + _tt * 256: ic * T + (_tt + 1) * 256],
                          fm=fm)
            else:
                for tc in range(2):
                    half = tc % 2
                    te.wait_ge(sSC, cSC.v)
                    for dc in range(8):
                        lhs = fm[:, dc * 256 + tc * 128: dc * 256 + (tc + 1) * 128]
                        te.matmul(PSA[:, half * 1024:half * 1024 + 512],
                                  lhs, WB[:, dc * D:dc * D + 512],
                                  start=(dc == 0), stop=(dc == 7))
                        te.matmul(PSA[:, half * 1024 + 512:half * 1024 + 1024],
                                  lhs, WB[:, dc * D + 512:dc * D + 1024],
                                  start=(dc == 0), stop=(dc == 7))
                    te.mark(sTE, cTE)
                    kc = tt * 2 + tc
                    sc.wait_ge(sTE, cTE.v)
                    sc.copy(V_sb.ap().rearrange("p (k h j) -> p k h j", k=NKC, h=H)
                            [:, kc, :, 0:64],
                            PSA.ap().rearrange("p (g h j) -> p g h j", g=2, h=H)[:, half]
                            ).then_inc(sSC, cSC.add(1))

    nc.all_engine_barrier()

    xt_free_ve = 0
    ln_ready_ve = 0
    wb_free_te = 0
    gp.wait_ge(sTE, cTE.v)
    gp.dma_start(out=WB.ap().rearrange("p (dc i) -> p dc i", dc=8),
                 in_=wq_d.rearrange("(dc p) i -> p dc i", p=128)).then_inc(sGD, cGD.add(16))
    wq_gd = cGD.v
    sync.wait_ge(sVE, cVE.v)
    sync.dma_start(out=xt_ap, in_=x_t[0]).then_inc(sLD, cLD.add(16))
    ve.wait_ge(sLD, cLD.v)
    emit_ln_to_fm(LNFM, srcb=XT)
    for m in range(NQB):
        nkc = 2 * m + 2
        XTc = XT
        if m == 0:
            te.wait_ge(sGD, wq_gd)
            proj_1024(lambda ic: QF[:, ic * 256:(ic + 1) * 256])
        else:
            sync.wait_ge(sVE, xt_free_ve)
            sync.dma_start(out=xt_ap, in_=x_t[m]).then_inc(sLD, cLD.add(16))
        q_done_sc = cSC.v

        te.wait_ge(sSC, q_done_sc)
        te.wait_ge(sVE, cVE.v)
        sc.wait_ge(sVE, cVE.v)
        mhalf_ve = [0, 0]
        mhalf_sc = [0, 0]
        tvexp_sc = 0
        odd_scr_ve = [0, 0]
        ph = 0
        for hp in range(8):
            ic = hp
            h_e, h_o = 2 * hp, 2 * hp + 1
            for qt in range(2):
                qtg = m * 2 + qt
                k0 = 128 * qtg
                npc = (T - k0 + 511) // 512
                w0 = min(512, T - k0)
                lhs_e = QF[0:64, ic * 256 + qt * 128: ic * 256 + (qt + 1) * 128]
                lhs_o = QF[64:128, ic * 256 + qt * 128: ic * 256 + (qt + 1) * 128]
                for pz in range(npc):
                    a = k0 + pz * 512
                    b = min(a + 512, T)
                    half = ph % 2
                    ph += 1
                    if mhalf_ve[half]:
                        te.wait_ge(sVE, mhalf_ve[half])
                    if mhalf_sc[half]:
                        te.wait_ge(sSC, mhalf_sc[half])
                    te.matmul(PSA[:, half * 1024: half * 1024 + (b - a)],
                              lhs_e, K_sb[0:64, ic * T + a: ic * T + b],
                              start=True, stop=True, tile_position=(0, 0))
                    te.matmul(PSA[:, half * 1024 + 512: half * 1024 + 512 + (b - a)],
                              lhs_o, K_sb[64:128, ic * T + a: ic * T + b],
                              start=True, stop=True, tile_position=(64, 0))
                    te.mark(sTE, cTE)
                    ve.wait_ge(sTE, cTE.v)
                    if pz == 0:
                        ve.wait_ge(sVE, cVE.v)
                    ve.reduce_max(PMAX[:, qt * 8 + pz:qt * 8 + pz + 1],
                                  PSA[:, half * 1024: half * 1024 + (b - a)],
                                  axis=AX.X).then_inc(sVE, cVE.add(1))
                    mhalf_ve[half] = cVE.v
                    spar = pz % 2
                    scr_o = spar * 512
                    sc.wait_ge(sTE, cTE.v)
                    if odd_scr_ve[spar]:
                        sc.wait_ge(sVE, odd_scr_ve[spar])
                    sc.activation(PB[:, scr_o:scr_o + (b - a)],
                                  PSA[:, half * 1024 + 512: half * 1024 + 512 + (b - a)],
                                  ACT.Exp, scale=CST[:, 2:3]).then_inc(sSC, cSC.add(1))
                    mhalf_sc[half] = cSC.v
                    ve.wait_ge(sSC, cSC.v)
                    ve.wait_ge(sVE, cVE.v)
                    if pz == 0:
                        ve.tensor_copy(PB[:, 1536:1536 + (b - a)],
                                       PB[:, scr_o:scr_o + (b - a)]
                                       ).then_inc(sVE, cVE.add(1))
                    else:
                        ve.tensor_max(PB[:, 1536:1536 + (b - a)],
                                      PB[:, 1536:1536 + (b - a)],
                                      PB[:, scr_o:scr_o + (b - a)]
                                      ).then_inc(sVE, cVE.add(1))
                    odd_scr_ve[spar] = cVE.v
                ve.wait_ge(sVE, cVE.v)
                if tvexp_sc:
                    ve.wait_ge(sSC, tvexp_sc)
                ve.reduce_max(MQ[:, qt:qt + 1], PMAX[:, qt * 8:qt * 8 + npc],
                              axis=AX.X)
                ve.reduce_max(TV[:, h_o * 2 + qt:h_o * 2 + qt + 1],
                              PB[:, 1536:1536 + w0], axis=AX.X)
                ve.mark(sVE, cVE)
                sc.wait_ge(sVE, cVE.v)
                sc.activation(TV[:, (h_e * 2 + qt):(h_e * 2 + qt) + 1],
                              MQ[:, qt:qt + 1], ACT.Exp, scale=CST[:, 2:3]
                              ).then_inc(sSC, cSC.add(1))
                tvexp_sc = cSC.v
        tvexp_all_sc = tvexp_sc

        if m + 1 < NQB:
            sync.wait_ge(sVE, cVE.v)
            sync.wait_ge(sST, cST.v)
            sync.dma_start(out=x2s_ap, in_=x_t[m + 1]).then_inc(sLX, cLX.add(16))
            ve.wait_ge(sLX, cLX.v)
            ve.wait_ge(sTE, cTE.v)
            ve.wait_ge(sSC, cSC.v)
            ln_pair(LNTM[:, :], LNTMB[:, :], dum=PB, srcb=X2S)
            ln_ready_ve = cVE.v

        psa_half_sc = [0, 0]
        runmax_ve = [cVE.v, cVE.v]
        num_free_sc = [0, 0]
        pend = None
        cmaxred_ve = 0
        mhtrans_te = 0
        te.wait_ge(sVE, cVE.v)

        def emit_pending():
            nonlocal pend, cmaxred_ve, mhtrans_te
            if pend is None:
                return
            hpp, comb_mark = pend
            pend = None
            te.wait_ge(sVE, comb_mark)
            if cmaxred_ve:
                te.wait_ge(sVE, cmaxred_ve)
            for par in range(2):
                for qt in range(2):
                    te.transpose(PSB16[:, par * 256 + qt * 128: par * 256 + (qt + 1) * 128],
                                 MH[:, par * 512 + qt * 128: par * 512 + (qt + 1) * 128],
                                 IDBF[:, :])
            te.mark(sTE, cTE)
            mhtrans_te = cTE.v
            ve.wait_ge(sTE, cTE.v)
            for par in range(2):
                for qt in range(2):
                    col = (2 * hpp + par) * 2 + qt
                    ve.reduce_max(CMAX[:, col:col + 1],
                                  PSB16[:, par * 256 + qt * 128: par * 256 + (qt + 1) * 128],
                                  axis=AX.X)
            ve.mark(sVE, cVE)
            cmaxred_ve = cVE.v

        for hp in range(8):
            ic = hp
            h_e, h_o = 2 * hp, 2 * hp + 1
            rhs_e = QF[0:64, ic * 256: ic * 256 + 256]
            rhs_o = QF[64:128, ic * 256: ic * 256 + 256]
            emit_pending()
            ngr = (nkc + 1) // 2
            qk_mark = {}
            exp_mark = {}
            vm_mark = {}
            pv_mark = {}

            def emit_qk(g):
                kcs = list(range(g * 2, min(g * 2 + 2, nkc)))
                half = g % 2
                if psa_half_sc[half]:
                    te.wait_ge(sSC, psa_half_sc[half])
                for j, kc in enumerate(kcs):
                    te.matmul(PSA[:, half * 1024 + j * 256: half * 1024 + (j + 1) * 256],
                              K_sb[0:64, ic * T + kc * 128: ic * T + (kc + 1) * 128],
                              rhs_e, start=True, stop=True, tile_position=(0, 0))
                    te.matmul(PSA[:, half * 1024 + 512 + j * 256: half * 1024 + 512 + (j + 1) * 256],
                              K_sb[64:128, ic * T + kc * 128: ic * T + (kc + 1) * 128],
                              rhs_o, start=True, stop=True, tile_position=(64, 0))
                te.mark(sTE, cTE)
                qk_mark[g] = cTE.v

            def emit_sv(g):
                kcs = list(range(g * 2, min(g * 2 + 2, nkc)))
                nk = len(kcs)
                assert nk == 2
                half = g % 2
                sc.wait_ge(sTE, qk_mark[g])
                if (g - 2) in pv_mark:
                    sc.wait_ge(sTE, pv_mark[g - 2])
                if runmax_ve[half]:
                    sc.wait_ge(sVE, runmax_ve[half])
                sc.activation(PB[:, half * 1024: half * 1024 + 1024],
                              PSA[:, half * 1024: half * 1024 + 1024],
                              ACT.Exp, scale=CST[:, 2:3]).then_inc(sSC, cSC.add(1))
                exp_mark[g] = cSC.v
                psa_half_sc[half] = cSC.v
                ve.wait_ge(sSC, exp_mark[g])
                if g == m:
                    ve.tensor_mul(PB[:, half * 1024: half * 1024 + 512],
                                  PB[:, half * 1024: half * 1024 + 512],
                                  TRI[:, 0:512])
                    ve.tensor_mul(PB[:, half * 1024 + 512: half * 1024 + 1024],
                                  PB[:, half * 1024 + 512: half * 1024 + 1024],
                                  TRI[:, 0:512])
                    ve.mark(sVE, cVE)
                ve.wait_ge(sVE, cVE.v)
                if g == 0:
                    if mhtrans_te:
                        ve.wait_ge(sTE, mhtrans_te)
                    ve.tensor_copy(MH[:, 0:1024],
                                   PB[:, half * 1024: half * 1024 + 1024])
                else:
                    ve.tensor_max(MH[:, 0:1024], MH[:, 0:1024],
                                  PB[:, half * 1024: half * 1024 + 1024])
                ve.mark(sVE, cVE)
                vm_mark[g] = cVE.v
                runmax_ve[half] = cVE.v

            def emit_pv(g):
                kcs = list(range(g * 2, min(g * 2 + 2, nkc)))
                half = g % 2
                te.wait_ge(sVE, vm_mark[g])
                if g == 0:
                    if num_free_sc[0]:
                        te.wait_ge(sSC, num_free_sc[0])
                    if num_free_sc[1]:
                        te.wait_ge(sSC, num_free_sc[1])
                for j, kc in enumerate(kcs):
                    te.matmul(PSN[0:65, 0:256],
                              V_sb[:, kc * (H * 65) + h_e * 65: kc * (H * 65) + h_e * 65 + 65],
                              PB[:, half * 1024 + j * 256: half * 1024 + (j + 1) * 256],
                              start=(kc == 0), stop=(kc == nkc - 1))
                    te.matmul(PSC[0:65, 0:256],
                              V_sb[:, kc * (H * 65) + h_o * 65: kc * (H * 65) + h_o * 65 + 65],
                              PB[:, half * 1024 + 512 + j * 256: half * 1024 + 512 + (j + 1) * 256],
                              start=(kc == 0), stop=(kc == nkc - 1))
                te.mark(sTE, cTE)
                pv_mark[g] = cTE.v
                return cTE.v

            emit_qk(0)
            if ngr > 1:
                emit_qk(1)
            emit_sv(0)
            if ngr > 2:
                emit_qk(2)
            if ngr > 1:
                emit_sv(1)
            pv_last = None
            for g in range(ngr):
                pv_last = emit_pv(g)
                if g + 3 < ngr:
                    emit_qk(g + 3)
                if g + 2 < ngr:
                    emit_sv(g + 2)
            sc.wait_ge(sTE, pv_last)
            sc.copy(NUMBUF[0:65, h_e * 256:(h_e + 1) * 256], PSN[0:65, 0:256])
            sc.copy(NUMBUF[0:65, h_o * 256:(h_o + 1) * 256],
                    PSC[0:65, 0:256]).then_inc(sSC, cSC.add(1))
            num_free_sc[0] = cSC.v
            num_free_sc[1] = cSC.v
            ve.wait_ge(sVE, cVE.v)
            ve.tensor_max(MH[:, 0:256], MH[:, 0:256], MH[:, 256:512])
            ve.tensor_max(MH[:, 512:768], MH[:, 512:768],
                          MH[:, 768:1024]).then_inc(sVE, cVE.add(1))
            pend = (hp, cVE.v)
        emit_pending()

        if m + 1 < NQB:
            te.wait_ge(sVE, cVE.v)
            for sub, src in ((0, LNTM[:, :]), (1, LNTMB[:, :])):
                te.wait_ge(sSC, cSC.v)
                for c in range(8):
                    te.transpose(PSB16[:, c * 128:(c + 1) * 128],
                                 src[:, c * 128:(c + 1) * 128], IDBF[:, :])
                te.mark(sTE, cTE)
                sc.wait_ge(sTE, cTE.v)
                dst = LNFM.ap().rearrange("p (dc t) -> p dc t", dc=8)[
                    :, :, sub * 128:(sub + 1) * 128]
                sc.copy(dst, PSB16.ap().rearrange("p (dc t) -> p dc t", dc=8)
                        ).then_inc(sSC, cSC.add(1))

        ve.wait_ge(sSC, tvexp_all_sc)
        ve.wait_ge(sVE, cVE.v)
        ve.tensor_max(TV[:, 0:32], TV[:, 0:32], CMAX[:, 0:32])
        ve.mark(sVE, cVE)
        te.wait_ge(sSC, cSC.v)
        te.wait_ge(sVE, cVE.v)
        for h in range(H):
            for qt in range(2):
                te.transpose(PSB16[:, 512 + (h * 2 + qt) * 2: 513 + (h * 2 + qt) * 2],
                             NUMBUF[64:65, h * 256 + qt * 128: h * 256 + (qt + 1) * 128],
                             IDBF[64:65, 64:65])
        te.mark(sTE, cTE)
        ve.wait_ge(sTE, cTE.v)
        den_ap = PSB16.ap().rearrange("p (c two) -> p c two", two=2)[:, 256:288, 0:1]
        tv_ap = TV.ap().rearrange("p (c one) -> p c one", one=1)[:, 0:32, 0:1]
        scr_ap = SCR.ap().rearrange("p (c one) -> p c one", one=1)[:, 48:80, 0:1]
        ve.scalar_tensor_tensor(scr_ap, tv_ap, EPS_ATTN,
                                den_ap, ALU.mult, ALU.add)
        ve.mark(sVE, cVE)
        ve.wait_ge(sVE, cVE.v)
        ve.reciprocal(SCR[:, 80:112], SCR[:, 48:80])
        ve.mark(sVE, cVE)
        recip_ve = cVE.v
        sc.wait_ge(sST, cST.v)
        sc.wait_ge(sVE, ln_ready_ve)
        ve.wait_ge(sST, cST.v)
        te.wait_ge(sVE, recip_ve)
        amult_ve = {}
        evac_sc = {}
        bc_mark = {}
        for rnd in range(2):
            if rnd == 1:
                te.wait_ge(sSC, cSC.v)
            for hl in range(8):
                h = rnd * 8 + hl
                for qt in range(2):
                    te.transpose(PSA[0:1, hl * 256 + qt * 128: hl * 256 + (qt + 1) * 128],
                                 SCR[:, 80 + h * 2 + qt:81 + h * 2 + qt], IDF[:, :])
            te.mark(sTE, cTE)
            rt_mark = cTE.v
            for hl in range(8):
                h = rnd * 8 + hl
                par = h % 2
                numps = PSN if par == 0 else PSC
                slot = (h // 2) % 2
                sslot = h % 2
                sc.wait_ge(sTE, rt_mark)
                if h >= 2:
                    sc.wait_ge(sTE, cTE.v)
                sc.copy(X2S[0:1, 1024 + sslot * 256: 1024 + (sslot + 1) * 256],
                        PSA[0:1, hl * 256:(hl + 1) * 256]).then_inc(sSC, cSC.add(1))
                evac_sc[h] = cSC.v
                te.wait_ge(sSC, evac_sc[h])
                if h >= 4 and (h - 4) in amult_ve:
                    te.wait_ge(sVE, amult_ve[h - 4])
                te.matmul(numps[0:64, slot * 256:(slot + 1) * 256],
                          ONESF[0:1, 0:64],
                          X2S[0:1, 1024 + sslot * 256: 1024 + (sslot + 1) * 256],
                          start=True, stop=True)
                te.mark(sTE, cTE)
                bc_mark[h] = cTE.v
                ve.wait_ge(sTE, cTE.v)
                if par == 0:
                    dst = ATTN[0:64, (h // 2) * 256:(h // 2 + 1) * 256]
                else:
                    dst = ATTS[0:64, (h // 2) * 256:(h // 2 + 1) * 256]
                ve.scalar_tensor_tensor(dst, NUMBUF[0:64, h * 256:(h + 1) * 256], 1.0,
                                        numps[0:64, slot * 256:(slot + 1) * 256],
                                        ALU.mult, ALU.mult).then_inc(sVE, cVE.add(1))
                amult_ve[h] = cVE.v
                if par == 1:
                    sync.wait_ge(sVE, amult_ve[h])
                    sync.dma_start(out=ATTN[64:128, (h // 2) * 256:(h // 2 + 1) * 256],
                                   in_=ATTS[0:64, (h // 2) * 256:(h // 2 + 1) * 256]
                                   ).then_inc(sSH, cSH.add(16))

        if m + 1 < NQB:
            te.wait_ge(sSC, cSC.v)
            te.wait_ge(sGD, cGD.v)
            pmarks = proj_1024(lambda ic: QF[:, ic * 256:(ic + 1) * 256])
        else:
            pmarks = [wb_free_te] * 8
        wo_src = wo_d.rearrange("(dc p) i -> p dc i", p=128)
        wb_dst = WB.ap().rearrange("p (dc i) -> p dc i", dc=8)
        for icw in range(8):
            gp.wait_ge(sTE, pmarks[icw])
            gp.dma_start(out=wb_dst[:, :, icw * 128:(icw + 1) * 128],
                         in_=wo_src[:, :, icw * 128:(icw + 1) * 128]
                         ).then_inc(sGD, cGD.add(16))

        te.wait_ge(sSH, cSH.v)
        te.wait_ge(sVE, cVE.v)
        te.wait_ge(sGD, cGD.v)
        te.wait_ge(sSC, cSC.v)
        for sub in range(2):
            for dh2 in range(2):
                for icc in range(8):
                    te.matmul(PSA[:, sub * 1024 + dh2 * 512: sub * 1024 + (dh2 + 1) * 512],
                              ATTN[:, icc * 256 + sub * 128: icc * 256 + (sub + 1) * 128],
                              WB[:, icc * D + dh2 * 512: icc * D + (dh2 + 1) * 512],
                              start=(icc == 0), stop=(icc == 7))
        te.mark(sTE, cTE)
        wb_free_te = cTE.v
        ve.wait_ge(sTE, cTE.v)
        ve.wait_ge(sLD, cLD.v)
        ve.wait_ge(sST, cST.v)
        ve.tensor_add(XTc[:, 0:D], XTc[:, 0:D], BOB[:, :])
        ve.tensor_add(XTc[:, D:2 * D], XTc[:, D:2 * D], BOB[:, :]).then_inc(sVE, cVE.add(1))
        ve.wait_ge(sVE, cVE.v)
        for sub in range(2):
            for dh2 in range(2):
                ve.scalar_tensor_tensor(
                    X2S[:, sub * 1024 + dh2 * 512: sub * 1024 + (dh2 + 1) * 512],
                    PSA[:, sub * 1024 + dh2 * 512: sub * 1024 + (dh2 + 1) * 512],
                    1.0, XTc[:, sub * D + dh2 * 512: sub * D + (dh2 + 1) * 512],
                    ALU.mult, ALU.add)
        ve.mark(sVE, cVE)
        xt_free_ve = cVE.v
        sync.wait_ge(sVE, cVE.v)
        sync.dma_start(out=x2_t[m], in_=x2s_ap).then_inc(sST, cST.add(16))
        if m + 2 < NQB:
            gp.wait_ge(sTE, cTE.v)
            gp.dma_start(out=WB.ap().rearrange("p (dc i) -> p dc i", dc=8),
                         in_=wq_d.rearrange("(dc p) i -> p dc i", p=128)
                         ).then_inc(sGD, cGD.add(16))

    nc.all_engine_barrier()

    wet_r = wet_d.rearrange("e (hc p) f -> p e hc f", p=128)
    wt_r = WT.ap().rearrange("p (e hc f) -> p e hc f", e=E, hc=8)
    for e in range(E):
        gp.dma_start(out=wt_r[:, e], in_=wet_r[:, e]).then_inc(sGD, cGD.add(16))
    a2_store = cST.v
    bsets = [(XT, LN2TM, LN2F32, MBC, sLD, cLD),
             (XTB2, LN2TMb, LN2F32, MBC, sLX, cLX)]
    hdr_ve = {}
    hdr_te = {}
    mask_ve = {}
    fin_ve = {}

    def hdrA(tt):
        XTb, TMb, F32b, MBCb_, sem, ct = bsets[tt % 2]
        if tt >= 2:
            sync.wait_ge(sVE, fin_ve.get(tt - 2, 0))
        sync.wait_ge(sST, a2_store)
        sync.dma_start(out=XTb.ap().rearrange("p (s c) -> p s c", s=2),
                       in_=x2_t[tt]).then_inc(sem, ct.add(16))
        ve.wait_ge(sem, ct.v)
        if tt >= 2:
            ve.wait_ge(sVE, cVE.v)
        ln_pair(TMb[:, 0:D], TMb[:, D:2 * D], dum=DUMB, srcb=XTb)
        hdr_ve[tt] = cVE.v

    def hdrB(tt):
        XTb, TMb, F32b, MBCb_, sem, ct = bsets[tt % 2]
        te.wait_ge(sVE, hdr_ve[tt])
        for sub in range(2):
            src = TMb[:, sub * D:(sub + 1) * D]
            for rnd in range(2):
                te.wait_ge(sSC, cSC.v)
                for c in range(4):
                    dc = rnd * 4 + c
                    te.transpose(PSB[:, c * 128:(c + 1) * 128],
                                 src[:, dc * 128:(dc + 1) * 128], IDF[:, :])
                te.mark(sTE, cTE)
                sc.wait_ge(sTE, cTE.v)
                dst = F32b.ap().rearrange("p (dc t) -> p dc t", dc=8)[
                    :, rnd * 4:(rnd + 1) * 4, sub * 128:(sub + 1) * 128]
                sc.copy(dst, PSB.ap().rearrange("p (c t) -> p c t", c=4)
                        ).then_inc(sSC, cSC.add(1))
        te.wait_ge(sSC, cSC.v)
        for sub in range(2):
            for hc in range(8):
                te.matmul(PSB[:, sub * 8:(sub + 1) * 8],
                          F32b[:, hc * 256 + sub * 128: hc * 256 + (sub + 1) * 128],
                          GWT[:, hc * 8:(hc + 1) * 8],
                          start=(hc == 0), stop=(hc == 7))
        te.mark(sTE, cTE)
        ve.wait_ge(sTE, cTE.v)
        if tt >= 1:
            ve.wait_ge(sVE, cVE.v)
        for sub in range(2):
            ve.reduce_max(MQ[:, sub:sub + 1], PSB[:, sub * 8:(sub + 1) * 8], axis=AX.X)
        ve.mark(sVE, cVE)
        ve.wait_ge(sVE, cVE.v)
        for sub in range(2):
            ve.tensor_scalar(EQ8[:, sub * 8:(sub + 1) * 8], PSB[:, sub * 8:(sub + 1) * 8],
                             MQ[:, sub:sub + 1], None, ALU.is_equal)
        ve.mark(sVE, cVE)
        ve.wait_ge(sVE, cVE.v)
        for sub in range(2):
            ve.scalar_tensor_tensor(SCR[:, 32 + sub * 8: 40 + sub * 8],
                                    EQ8[:, sub * 8:(sub + 1) * 8], -16.0,
                                    IOTA[:, 8:16], ALU.mult, ALU.add)
        ve.mark(sVE, cVE)
        ve.wait_ge(sVE, cVE.v)
        for sub in range(2):
            ve.tensor_reduce(IDX[:, sub:sub + 1], SCR[:, 32 + sub * 8: 40 + sub * 8],
                             AX.X, ALU.min)
        ve.mark(sVE, cVE)
        te.wait_ge(sVE, cVE.v)
        for sub in range(2):
            te.transpose(PSB[0:1, 16 + sub * 128: 16 + (sub + 1) * 128],
                         IDX[:, sub:sub + 1], IDF[:, :])
        te.mark(sTE, cTE)
        sc.wait_ge(sTE, cTE.v)
        sc.wait_ge(sST, cST.v)
        sc.copy(X2S[0:1, 1024:1280], PSB[0:1, 16:272]).then_inc(sSC, cSC.add(1))
        te.wait_ge(sSC, cSC.v)
        if tt >= 1:
            te.wait_ge(sVE, mask_ve.get(tt - 1, 0))
        te.matmul(PSC[:, 0:256], ONESF[0:1, 0:128], X2S[0:1, 1024:1280],
                  start=True, stop=True)
        te.mark(sTE, cTE)
        ve.wait_ge(sTE, cTE.v)
        ve.wait_ge(sVE, cVE.v)
        for e in range(E):
            ve.tensor_scalar(MBCb_[:, e * 256:(e + 1) * 256], PSC[:, 0:256],
                             float(e), None, ALU.is_equal)
        ve.mark(sVE, cVE)
        mask_ve[tt] = cVE.v
        hdr_te[tt] = cTE.v

    hdrA(0)
    hdrB(0)
    for tt in range(NQB):
        XTb, TMb, F32b, MBCb_, _, _ = bsets[tt % 2]
        if tt + 1 < NQB:
            hdrA(tt + 1)
        if tt == 0:
            te.wait_ge(sGD, cGD.v)
        NEH = E * 8
        mln_ve = {}
        mm_te = {}

        def emit_mln(i):
            e, hc = i // 8, i % 8
            buf = i % 2
            if (i - 2) in mm_te:
                ve.wait_ge(sTE, mm_te[i - 2])
            ve.tensor_mul(MLN[:, buf * 256:(buf + 1) * 256],
                          F32b[:, hc * 256:(hc + 1) * 256],
                          MBCb_[:, e * 256:(e + 1) * 256]).then_inc(sVE, cVE.add(1))
            mln_ve[i] = cVE.v

        ve.wait_ge(sVE, mask_ve[tt])
        ve.wait_ge(sTE, cTE.v)
        emit_mln(0)
        for i in range(NEH):
            if i + 1 < NEH:
                emit_mln(i + 1)
            buf = i % 2
            te.wait_ge(sVE, mln_ve[i])
            if i == 0:
                te.wait_ge(sVE, fin_ve.get(tt - 1, 0))
            for sub in range(2):
                for fh in range(2):
                    te.matmul(PSA[:, sub * 1024 + fh * 512: sub * 1024 + (fh + 1) * 512],
                              MLN[:, buf * 256 + sub * 128: buf * 256 + (sub + 1) * 128],
                              WT[:, i * D + fh * 512: i * D + (fh + 1) * 512],
                              start=(i == 0), stop=(i == NEH - 1))
            te.mark(sTE, cTE)
            mm_te[i] = cTE.v
        ve.wait_ge(sTE, cTE.v)
        ve.wait_ge(sST, cST.v)
        for sub in range(2):
            for fh in range(2):
                ve.scalar_tensor_tensor(
                    X2S[:, sub * 1024 + fh * 512: sub * 1024 + (fh + 1) * 512],
                    PSA[:, sub * 1024 + fh * 512: sub * 1024 + (fh + 1) * 512],
                    1.0, TMb[:, sub * D + fh * 512: sub * D + (fh + 1) * 512],
                    ALU.mult, ALU.add)
        ve.mark(sVE, cVE)
        ve.wait_ge(sVE, cVE.v)
        for sub in range(2):
            for fh in range(2):
                ve.tensor_add(
                    X2S[:, sub * 1024 + fh * 512: sub * 1024 + (fh + 1) * 512],
                    X2S[:, sub * 1024 + fh * 512: sub * 1024 + (fh + 1) * 512],
                    XTb[:, sub * D + fh * 512: sub * D + (fh + 1) * 512])
        ve.mark(sVE, cVE)
        fin_ve[tt] = cVE.v
        sync.wait_ge(sVE, cVE.v)
        sync.dma_start(out=out_t[tt], in_=x2s_ap).then_inc(sST, cST.add(16))
        if tt + 1 < NQB:
            hdrB(tt + 1)

    sync.wait_ge(sST, cST.v)
    if dbg is not None:
        dbg.update(locals())
    return nc


def _consts():
    ident_bf = np.eye(128, dtype=ml_dtypes.bfloat16)
    ident_f = np.eye(128, dtype=np.float32)
    k = np.arange(128)[:, None]
    q = np.arange(128)[None, :]
    tri = (k <= q).astype(np.float32)
    tri0 = np.concatenate([tri, np.ones((128, 128), np.float32)], axis=1)
    tri1 = np.concatenate([np.zeros((128, 128), np.float32), tri], axis=1)
    tri_all = np.concatenate([tri0, tri1], axis=1).astype(ml_dtypes.bfloat16)
    iota = np.broadcast_to(np.arange(8, dtype=np.float32), (128, 8))
    iota16 = np.concatenate([iota, iota + 16.0], axis=1).astype(np.float32)
    ones_f = np.ones((1, 128), dtype=np.float32)
    cst = np.zeros((128, 8), dtype=np.float32)
    cst[:, 0] = EPS_LN
    cst[:, 1] = -0.5
    cst[:, 2] = SCALE
    return ident_bf, ident_f, tri_all, np.ascontiguousarray(iota16), ones_f, cst


def kernel(L0_attn_x, L0_attn_Wq, L0_attn_Wkv, L0_attn_Wo, L0_attn_bo,
           L0_ffn_gate_w, L0_ffn_expert_w):
    x = np.ascontiguousarray(np.asarray(L0_attn_x, dtype=np.float32))
    wq = np.ascontiguousarray(np.asarray(L0_attn_Wq, dtype=np.float32))
    wkv = np.ascontiguousarray(np.asarray(L0_attn_Wkv, dtype=np.float32))
    wo = np.ascontiguousarray(np.asarray(L0_attn_Wo, dtype=np.float32))
    bo = np.asarray(L0_attn_bo, dtype=np.float32)
    gw = np.asarray(L0_ffn_gate_w, dtype=np.float32)
    we = np.asarray(L0_ffn_expert_w, dtype=np.float32)
    B, T, _ = x.shape
    bo_b = np.ascontiguousarray(np.broadcast_to(bo[None, :], (128, D)))
    gwt = np.ascontiguousarray(gw.T)
    wet = np.ascontiguousarray(we.transpose(0, 2, 1))
    ident_bf, ident_f, tri_all, iota16, ones_f, cst = _consts()

    nc = build(T=T)
    shared = {
        "wq": wq, "wkv": wkv, "wo": wo, "bo_b": bo_b, "gwt": gwt, "wet": wet,
        "id_bf": ident_bf, "id_f32": ident_f, "tri": tri_all,
        "iota8": iota16, "ones_f": ones_f, "cst": cst,
    }
    in_maps = [dict(shared, x=np.ascontiguousarray(x[b])) for b in range(B)]
    res = run_bass_kernel_spmd(nc, in_maps, core_ids=list(range(B)))
    out = np.stack([np.asarray(r["out"]) for r in res.results], axis=0)
    return out.astype(np.float32)


# version 20
# speedup vs baseline: 1.1037x; 1.0058x over previous
import numpy as np
import ml_dtypes
import concourse.bass as bass
import concourse.mybir as mybir
from concourse.bass_utils import run_bass_kernel_spmd

F32 = mybir.dt.float32
BF16 = mybir.dt.bfloat16
AX = mybir.AxisListType
ALU = mybir.AluOpType
ACT = mybir.ActivationFunctionType

D = 1024
H = 16
DH = 64
E = 8
SCALE = DH ** -0.5
EPS_ATTN = 1e-8
EPS_LN = 1e-5



class EngW:

    def __init__(self, eng):
        self._e = eng
        self.last = None

    def __getattr__(self, name):
        f = getattr(self._e, name)
        if not callable(f):
            return f
        rec = name not in ("wait_ge", "nop")

        def w(*a, **k):
            r = f(*a, **k)
            if rec and hasattr(r, "then_inc"):
                self.last = r
            return r
        return w

    def mark(self, sem, ct):
        self.last.then_inc(sem, ct.add(1))


class Ct:
    def __init__(self):
        self.v = 0

    def add(self, n):
        self.v += n
        return n


def build(T=4096, dbg=None):
    QB = 256
    NQB = T // QB
    NKC = T // 128

    nc = bass.Bass("TRN2", target_bir_lowering=False)

    x_d = nc.dram_tensor("x", [T, D], F32, kind="ExternalInput").ap()
    wq_d = nc.dram_tensor("wq", [D, D], F32, kind="ExternalInput").ap()
    wkv_d = nc.dram_tensor("wkv", [D, 2 * D], F32, kind="ExternalInput").ap()
    wo_d = nc.dram_tensor("wo", [D, D], F32, kind="ExternalInput").ap()
    bo_d = nc.dram_tensor("bo_b", [128, D], F32, kind="ExternalInput").ap()
    gwt_d = nc.dram_tensor("gwt", [D, E], F32, kind="ExternalInput").ap()
    wet_d = nc.dram_tensor("wet", [E, D, D], F32, kind="ExternalInput").ap()
    idbf_d = nc.dram_tensor("id_bf", [128, 128], BF16, kind="ExternalInput").ap()
    idf32_d = nc.dram_tensor("id_f32", [128, 128], F32, kind="ExternalInput").ap()
    tri_d = nc.dram_tensor("tri", [128, 512], BF16, kind="ExternalInput").ap()
    iota_d = nc.dram_tensor("iota8", [128, 16], F32, kind="ExternalInput").ap()
    ones_d = nc.dram_tensor("ones_f", [1, 128], F32, kind="ExternalInput").ap()
    cst_d = nc.dram_tensor("cst", [128, 8], F32, kind="ExternalInput").ap()
    out_d = nc.dram_tensor("out", [T, D], F32, kind="ExternalOutput").ap()
    x2_d = nc.dram_tensor("x2scr", [T, D], F32, kind="Internal").ap()

    sLD = nc.alloc_semaphore("sLD")
    sLX = nc.alloc_semaphore("sLX")
    sLC = nc.alloc_semaphore("sLC")
    sGD = nc.alloc_semaphore("sGD")
    sST = nc.alloc_semaphore("sST")
    sSH = nc.alloc_semaphore("sSH")
    sTE = nc.alloc_semaphore("sTE")
    sSC = nc.alloc_semaphore("sSC")
    sVE = nc.alloc_semaphore("sVE")
    sGP = nc.alloc_semaphore("sGP")
    cLD, cLX, cLC, cGD, cST, cSH, cTE, cSC, cVE, cGP = (Ct() for _ in range(10))

    ARENA_BYTES = 210800
    _arena = nc.alloc_sbuf_tensor("ARENA", [128, ARENA_BYTES], mybir.dt.uint8)
    _base = nc.lookup_mloc(_arena).addr
    _off = [_base]

    def _at(name, shape, dtype, region_end=None):
        nbytes = shape[1] * (2 if dtype == BF16 else 4)
        addr = (_off[0] + 31) & ~31
        h = nc.alloc_sbuf_tensor_at(name, shape, dtype, offset=addr)
        _off[0] = addr + nbytes
        return h

    XT = _at("XT", [128, 2 * D], F32)
    _x2s_off = (_off[0] + 31) & ~31
    X2S = _at("X2S", [128, 2048], F32)
    SCR = _at("SCR", [128, 128], F32)
    BOB = _at("BOB", [128, D], BF16)
    IDBF = _at("IDBF", [128, 128], BF16)
    IDF = _at("IDF", [128, 128], F32)
    TRI = _at("TRI", [128, 512], BF16)
    IOTA = _at("IOTA", [128, 16], F32)
    ONESF = _at("ONESF", [1, 128], F32)
    CST = _at("CST", [128, 8], F32)
    GWT = _at("GWT", [128, 64], F32)
    MQ = _at("MQ", [128, 4], F32)
    EQ8 = _at("EQ8", [128, 16], F32)
    IDX = _at("IDX", [128, 2], F32)
    TV = _at("TV", [128, 32], F32)
    PMAX = _at("PMAX", [128, 16], F32)
    MH = _at("MH", [128, 1024], BF16)
    CMAX = _at("CMAX", [128, 32], F32)
    LNTM = _at("LNTM", [128, D], BF16)
    LNTMB = _at("LNTMB", [128, D], BF16)
    ATTS = nc.alloc_sbuf_tensor_at("ATTS", [128, 2048], BF16, offset=_x2s_off)
    shared_end = _off[0]
    _k_off = (_off[0] + 31) & ~31
    WB = _at("WB", [128, 8 * D], BF16)
    K_sb = _at("K", [128, 8 * T], BF16)
    V_sb = _at("V", [128, NKC * (H * 65)], BF16)
    QF = _at("QF", [128, 2048], BF16)
    PB = _at("PB", [128, 2048], BF16)
    ATTN = _at("ATTN", [128, 2048], BF16)
    LNFM = _at("LNFM", [128, 2048], BF16)
    NUMBUF = _at("NUMBUF", [128, 4096], BF16)
    LNFM2 = _at("LNFM2", [128, 2048], BF16)
    LNTM2 = _at("LNTM2", [128, D], BF16)
    LNTMB2 = _at("LNTMB2", [128, D], BF16)
    a_end = _off[0]
    _off[0] = shared_end
    WT = _at("WT", [128, 64 * D], BF16)
    LN2TM = _at("LN2TM", [128, 2048], F32)
    LN2F32 = _at("LN2F32", [128, 2048], F32)
    MBC = _at("MBC", [128, 2048], BF16)
    XTB2 = _at("XTB2", [128, 2048], F32)
    LN2TMb = _at("LN2TMb", [128, 2048], F32)
    DUMB = _at("DUMB", [128, 2048], BF16)
    MLN = _at("MLN", [128, 1024], BF16)
    b_end = _off[0]
    assert max(a_end, b_end) < _base + ARENA_BYTES, (_base, shared_end, a_end, b_end)
    LN2F16 = LN2F32
    PSA = nc.alloc_psum_tensor("PSA", [128, 2048], F32)
    PSB = nc.alloc_psum_tensor("PSB", [128, 512], F32)
    PSB16 = nc.alloc_psum_tensor("PSB16", [128, 1024], BF16)
    PSN = nc.alloc_psum_tensor("PSN", [128, 512], F32)
    PSC = nc.alloc_psum_tensor("PSC", [128, 512], F32)

    sync, gp, te, sc, ve = (EngW(nc.sync), EngW(nc.gpsimd), EngW(nc.tensor),
                            EngW(nc.scalar), EngW(nc.vector))

    sync.dma_start(out=IDBF[:, :], in_=idbf_d).then_inc(sLC, cLC.add(16))
    sync.dma_start(out=IDF[:, :], in_=idf32_d).then_inc(sLC, cLC.add(16))
    sync.dma_start(out=TRI[:, :], in_=tri_d).then_inc(sLC, cLC.add(16))
    sync.dma_start(out=IOTA[:, :], in_=iota_d).then_inc(sLC, cLC.add(16))
    sync.dma_start(out=ONESF[:, :], in_=ones_d).then_inc(sLC, cLC.add(16))
    sync.dma_start(out=CST[:, :], in_=cst_d).then_inc(sLC, cLC.add(16))
    gp.dma_start(out=BOB[:, :], in_=bo_d).then_inc(sGD, cGD.add(16))
    sync.dma_start(out=GWT.ap().rearrange("p (hc e) -> p hc e", hc=8),
                   in_=gwt_d.rearrange("(hc p) e -> p hc e", p=128)).then_inc(sLC, cLC.add(16))
    gp.dma_start(out=WB.ap().rearrange("p (dc i) -> p dc i", dc=8),
                 in_=wkv_d.rearrange("(dc p) i -> p dc i", p=128)[:, :, 0:D]
                 ).then_inc(sGD, cGD.add(16))
    ve.wait_ge(sLC, cLC.v)
    sc.wait_ge(sLC, cLC.v)
    te.wait_ge(sLC, cLC.v)
    ve.memset(V_sb.ap().rearrange("p (k h j) -> p k h j", k=NKC, h=H)[:, :, :, 64:65], 1.0)
    ve.mark(sVE, cVE)

    x_t = x_d.rearrange("(n s p) c -> n p s c", s=2, p=128)
    x2_t = x2_d.rearrange("(n s p) c -> n p s c", s=2, p=128)
    out_t = out_d.rearrange("(n s p) c -> n p s c", s=2, p=128)
    xt_ap = XT.ap().rearrange("p (s c) -> p s c", s=2)
    x2s_ap = X2S.ap().rearrange("p (s c) -> p s c", s=2)

    def ln_pair(out0, out1, dum=None, srcb=None):
        cols = [SCR[:, c:c + 1] for c in range(16)]
        (s_sum0, s_ss0, s_m0, s_e0, s_v0, s_l0, s_r0, _,
         s_sum1, s_ss1, s_m1, s_e1, s_v1, s_l1, s_r1, _) = cols
        if srcb is None:
            srcb = XT
        src0 = srcb[:, 0:D]
        src1 = srcb[:, D:2 * D]
        ve.reduce_sum(s_sum0, src0, axis=AX.X)
        ve.reduce_sum(s_sum1, src1, axis=AX.X)
        ve.scalar_tensor_tensor(dum[:, 0:D], src0, 1.0, src0, ALU.mult, ALU.mult,
                                accum_out=s_ss0)
        ve.scalar_tensor_tensor(dum[:, D:2 * D], src1, 1.0, src1, ALU.mult, ALU.mult,
                                accum_out=s_ss1).then_inc(sVE, cVE.add(1))
        ve.wait_ge(sVE, cVE.v)
        ve.tensor_scalar_mul(s_m0, s_sum0, 1.0 / D)
        ve.tensor_scalar_mul(s_m1, s_sum1, 1.0 / D)
        ve.tensor_scalar_mul(s_e0, s_ss0, 1.0 / D)
        ve.tensor_scalar_mul(s_e1, s_ss1, 1.0 / D).then_inc(sVE, cVE.add(1))
        ve.wait_ge(sVE, cVE.v)
        ve.tensor_mul(s_v0, s_m0, s_m0)
        ve.tensor_mul(s_v1, s_m1, s_m1).then_inc(sVE, cVE.add(1))
        ve.wait_ge(sVE, cVE.v)
        ve.tensor_sub(s_v0, s_e0, s_v0)
        ve.tensor_sub(s_v1, s_e1, s_v1).then_inc(sVE, cVE.add(1))
        sc.wait_ge(sVE, cVE.v)
        sc.activation(s_l0, s_v0, ACT.Ln, bias=CST[:, 0:1])
        sc.activation(s_l1, s_v1, ACT.Ln, bias=CST[:, 0:1]).then_inc(sSC, cSC.add(1))
        sc.wait_ge(sSC, cSC.v)
        sc.activation(s_r0, s_l0, ACT.Exp, scale=CST[:, 1:2])
        sc.activation(s_r1, s_l1, ACT.Exp, scale=CST[:, 1:2]).then_inc(sSC, cSC.add(1))
        ve.wait_ge(sSC, cSC.v)
        ve.tensor_scalar(out0, src0, s_m0[:, 0:1], s_r0[:, 0:1], ALU.subtract, ALU.mult)
        ve.tensor_scalar(out1, src1, s_m1[:, 0:1], s_r1[:, 0:1],
                         ALU.subtract, ALU.mult).then_inc(sVE, cVE.add(1))


    def emit_ln_to_fm(dst_fm, f32=False, dst32=None, srcb=None, tm0=None, tm1=None):
        if f32:
            ln_pair(dst32[:, 0:D], dst32[:, D:2 * D], dum=DUMB, srcb=srcb)
            src0, src1 = dst32[:, 0:D], dst32[:, D:2 * D]
        else:
            t0_, t1_ = (LNTM, LNTMB) if tm0 is None else (tm0, tm1)
            ln_pair(t0_[:, :], t1_[:, :], dum=PB, srcb=srcb)
            src0, src1 = t0_[:, :], t1_[:, :]
        te.wait_ge(sVE, cVE.v)
        for sub, src in ((0, src0), (1, src1)):
            if f32:
                for rnd in range(2):
                    te.wait_ge(sSC, cSC.v)
                    for c in range(4):
                        dc = rnd * 4 + c
                        te.transpose(PSB[:, c * 128:(c + 1) * 128],
                                     src[:, dc * 128:(dc + 1) * 128], IDF[:, :])
                    te.mark(sTE, cTE)
                    sc.wait_ge(sTE, cTE.v)
                    dst = dst_fm.ap().rearrange("p (dc t) -> p dc t", dc=8)[
                        :, rnd * 4:(rnd + 1) * 4, sub * 128:(sub + 1) * 128]
                    sc.copy(dst, PSB.ap().rearrange("p (c t) -> p c t", c=4)
                            ).then_inc(sSC, cSC.add(1))
            else:
                te.wait_ge(sSC, cSC.v)
                for c in range(8):
                    te.transpose(PSB16[:, c * 128:(c + 1) * 128],
                                 src[:, c * 128:(c + 1) * 128], IDBF[:, :])
                te.mark(sTE, cTE)
                sc.wait_ge(sTE, cTE.v)
                dst = dst_fm.ap().rearrange("p (dc t) -> p dc t", dc=8)[
                    :, :, sub * 128:(sub + 1) * 128]
                sc.copy(dst, PSB16.ap().rearrange("p (dc t) -> p dc t", dc=8)
                        ).then_inc(sSC, cSC.add(1))
        te.wait_ge(sSC, cSC.v)

    def proj_1024(dst_sb, rhs_cols=256, fm=None):
        if fm is None:
            fm = LNFM
        marks = []
        for ic in range(8):
            bank = ic % 4
            if ic >= 4:
                te.wait_ge(sSC, cSC.v)
            for dc in range(8):
                te.matmul(PSA[:, bank * 512:bank * 512 + 256],
                          WB[:, dc * D + ic * 128:dc * D + (ic + 1) * 128],
                          fm[:, dc * 256:(dc + 1) * 256],
                          start=(dc == 0), stop=(dc == 7))
            te.mark(sTE, cTE)
            marks.append(cTE.v)
            sc.wait_ge(sTE, cTE.v)
            sc.copy(dst_sb(ic), PSA[:, bank * 512:bank * 512 + 256]
                    ).then_inc(sSC, cSC.add(1))
        return marks

    p1_xb = [(XT, xt_ap, sLD, cLD), (X2S, x2s_ap, sLX, cLX)]
    p1_fm = [(LNFM, LNTM, LNTMB), (LNFM2, LNTM2, LNTMB2)]
    p1_xfree = [0, 0]
    sync.dma_start(out=xt_ap, in_=x_t[0]).then_inc(sLD, cLD.add(16))
    for pas in (0, 1):
        if pas == 1:
            gp.wait_ge(sTE, cTE.v)
            gp.dma_start(out=WB.ap().rearrange("p (dc i) -> p dc i", dc=8),
                         in_=wkv_d.rearrange("(dc p) i -> p dc i", p=128)[:, :, D:2 * D]
                         ).then_inc(sGD, cGD.add(16))
        for tt in range(NQB):
            g = pas * NQB + tt
            xb, xap, sem, ct = p1_xb[g % 2]
            fm, tm0, tm1 = p1_fm[g % 2]
            if g + 1 < 2 * NQB:
                xb2, xap2, sem2, ct2 = p1_xb[(g + 1) % 2]
                if p1_xfree[(g + 1) % 2]:
                    sync.wait_ge(sVE, p1_xfree[(g + 1) % 2])
                sync.dma_start(out=xap2, in_=x_t[(g + 1) % NQB]
                               ).then_inc(sem2, ct2.add(16))
            ve.wait_ge(sem, ct.v)
            if pas == 0 and tt == 0:
                ve.wait_ge(sGD, cGD.v)
            emit_ln_to_fm(fm, srcb=xb, tm0=tm0, tm1=tm1)
            p1_xfree[g % 2] = cVE.v
            te.wait_ge(sGD, cGD.v)
            if pas == 0:
                proj_1024(lambda ic, _tt=tt: K_sb[:, ic * T + _tt * 256: ic * T + (_tt + 1) * 256],
                          fm=fm)
            else:
                for tc in range(2):
                    half = tc % 2
                    te.wait_ge(sSC, cSC.v)
                    for dc in range(8):
                        lhs = fm[:, dc * 256 + tc * 128: dc * 256 + (tc + 1) * 128]
                        te.matmul(PSA[:, half * 1024:half * 1024 + 512],
                                  lhs, WB[:, dc * D:dc * D + 512],
                                  start=(dc == 0), stop=(dc == 7))
                        te.matmul(PSA[:, half * 1024 + 512:half * 1024 + 1024],
                                  lhs, WB[:, dc * D + 512:dc * D + 1024],
                                  start=(dc == 0), stop=(dc == 7))
                    te.mark(sTE, cTE)
                    kc = tt * 2 + tc
                    sc.wait_ge(sTE, cTE.v)
                    sc.copy(V_sb.ap().rearrange("p (k h j) -> p k h j", k=NKC, h=H)
                            [:, kc, :, 0:64],
                            PSA.ap().rearrange("p (g h j) -> p g h j", g=2, h=H)[:, half]
                            ).then_inc(sSC, cSC.add(1))

    nc.all_engine_barrier()

    xt_free_ve = 0
    ln_ready_ve = 0
    wb_free_te = 0
    gp.wait_ge(sTE, cTE.v)
    gp.dma_start(out=WB.ap().rearrange("p (dc i) -> p dc i", dc=8),
                 in_=wq_d.rearrange("(dc p) i -> p dc i", p=128)).then_inc(sGD, cGD.add(16))
    wq_gd = cGD.v
    sync.wait_ge(sVE, cVE.v)
    sync.dma_start(out=xt_ap, in_=x_t[0]).then_inc(sLD, cLD.add(16))
    ve.wait_ge(sLD, cLD.v)
    emit_ln_to_fm(LNFM, srcb=XT)
    for m in range(NQB):
        nkc = 2 * m + 2
        XTc = XT
        if m == 0:
            te.wait_ge(sGD, wq_gd)
            proj_1024(lambda ic: QF[:, ic * 256:(ic + 1) * 256])
        else:
            sync.wait_ge(sVE, xt_free_ve)
            sync.dma_start(out=xt_ap, in_=x_t[m]).then_inc(sLD, cLD.add(16))
        q_done_sc = cSC.v

        te.wait_ge(sSC, q_done_sc)
        te.wait_ge(sVE, cVE.v)
        sc.wait_ge(sVE, cVE.v)
        mhalf_ve = [0, 0]
        mhalf_sc = [0, 0]
        tvexp_sc = 0
        odd_scr_ve = [0, 0]
        ph = 0
        for hp in range(8):
            ic = hp
            h_e, h_o = 2 * hp, 2 * hp + 1
            for qt in range(2):
                qtg = m * 2 + qt
                k0 = 128 * qtg
                npc = (T - k0 + 511) // 512
                w0 = min(512, T - k0)
                lhs_e = QF[0:64, ic * 256 + qt * 128: ic * 256 + (qt + 1) * 128]
                lhs_o = QF[64:128, ic * 256 + qt * 128: ic * 256 + (qt + 1) * 128]
                for pz in range(npc):
                    a = k0 + pz * 512
                    b = min(a + 512, T)
                    half = ph % 2
                    ph += 1
                    if mhalf_ve[half]:
                        te.wait_ge(sVE, mhalf_ve[half])
                    if mhalf_sc[half]:
                        te.wait_ge(sSC, mhalf_sc[half])
                    te.matmul(PSA[:, half * 1024: half * 1024 + (b - a)],
                              lhs_e, K_sb[0:64, ic * T + a: ic * T + b],
                              start=True, stop=True, tile_position=(0, 0))
                    te.matmul(PSA[:, half * 1024 + 512: half * 1024 + 512 + (b - a)],
                              lhs_o, K_sb[64:128, ic * T + a: ic * T + b],
                              start=True, stop=True, tile_position=(64, 0))
                    te.mark(sTE, cTE)
                    ve.wait_ge(sTE, cTE.v)
                    if pz == 0:
                        ve.wait_ge(sVE, cVE.v)
                    ve.reduce_max(PMAX[:, qt * 8 + pz:qt * 8 + pz + 1],
                                  PSA[:, half * 1024: half * 1024 + (b - a)],
                                  axis=AX.X).then_inc(sVE, cVE.add(1))
                    mhalf_ve[half] = cVE.v
                    spar = pz % 2
                    scr_o = spar * 512
                    sc.wait_ge(sTE, cTE.v)
                    if odd_scr_ve[spar]:
                        sc.wait_ge(sVE, odd_scr_ve[spar])
                    sc.activation(PB[:, scr_o:scr_o + (b - a)],
                                  PSA[:, half * 1024 + 512: half * 1024 + 512 + (b - a)],
                                  ACT.Exp, scale=CST[:, 2:3]).then_inc(sSC, cSC.add(1))
                    mhalf_sc[half] = cSC.v
                    ve.wait_ge(sSC, cSC.v)
                    ve.wait_ge(sVE, cVE.v)
                    if pz == 0:
                        ve.tensor_copy(PB[:, 1536:1536 + (b - a)],
                                       PB[:, scr_o:scr_o + (b - a)]
                                       ).then_inc(sVE, cVE.add(1))
                    else:
                        ve.tensor_max(PB[:, 1536:1536 + (b - a)],
                                      PB[:, 1536:1536 + (b - a)],
                                      PB[:, scr_o:scr_o + (b - a)]
                                      ).then_inc(sVE, cVE.add(1))
                    odd_scr_ve[spar] = cVE.v
                ve.wait_ge(sVE, cVE.v)
                if tvexp_sc:
                    ve.wait_ge(sSC, tvexp_sc)
                ve.reduce_max(MQ[:, qt:qt + 1], PMAX[:, qt * 8:qt * 8 + npc],
                              axis=AX.X)
                ve.reduce_max(TV[:, h_o * 2 + qt:h_o * 2 + qt + 1],
                              PB[:, 1536:1536 + w0], axis=AX.X)
                ve.mark(sVE, cVE)
                sc.wait_ge(sVE, cVE.v)
                sc.activation(TV[:, (h_e * 2 + qt):(h_e * 2 + qt) + 1],
                              MQ[:, qt:qt + 1], ACT.Exp, scale=CST[:, 2:3]
                              ).then_inc(sSC, cSC.add(1))
                tvexp_sc = cSC.v
        tvexp_all_sc = tvexp_sc

        if m + 1 < NQB:
            sync.wait_ge(sVE, cVE.v)
            sync.wait_ge(sST, cST.v)
            sync.dma_start(out=x2s_ap, in_=x_t[m + 1]).then_inc(sLX, cLX.add(16))
            ve.wait_ge(sLX, cLX.v)
            ve.wait_ge(sTE, cTE.v)
            ve.wait_ge(sSC, cSC.v)
            ln_pair(LNTM[:, :], LNTMB[:, :], dum=PB, srcb=X2S)
            ln_ready_ve = cVE.v

        psa_half_sc = [0, 0]
        runmax_ve = [cVE.v, cVE.v]
        num_free_sc = [0, 0]
        pend = None
        cmaxred_ve = 0
        mhtrans_te = 0
        te.wait_ge(sVE, cVE.v)

        def emit_pending():
            nonlocal pend, cmaxred_ve, mhtrans_te
            if pend is None:
                return
            hpp, comb_mark = pend
            pend = None
            te.wait_ge(sVE, comb_mark)
            if cmaxred_ve:
                te.wait_ge(sVE, cmaxred_ve)
            for par in range(2):
                for qt in range(2):
                    te.transpose(PSB16[:, par * 256 + qt * 128: par * 256 + (qt + 1) * 128],
                                 MH[:, par * 512 + qt * 128: par * 512 + (qt + 1) * 128],
                                 IDBF[:, :])
            te.mark(sTE, cTE)
            mhtrans_te = cTE.v
            ve.wait_ge(sTE, cTE.v)
            for par in range(2):
                for qt in range(2):
                    col = (2 * hpp + par) * 2 + qt
                    ve.reduce_max(CMAX[:, col:col + 1],
                                  PSB16[:, par * 256 + qt * 128: par * 256 + (qt + 1) * 128],
                                  axis=AX.X)
            ve.mark(sVE, cVE)
            cmaxred_ve = cVE.v

        for hp in range(8):
            ic = hp
            h_e, h_o = 2 * hp, 2 * hp + 1
            rhs_e = QF[0:64, ic * 256: ic * 256 + 256]
            rhs_o = QF[64:128, ic * 256: ic * 256 + 256]
            emit_pending()
            ngr = (nkc + 1) // 2
            qk_mark = {}
            exp_mark = {}
            vm_mark = {}
            pv_mark = {}

            def emit_qk(g):
                kcs = list(range(g * 2, min(g * 2 + 2, nkc)))
                half = g % 2
                if psa_half_sc[half]:
                    te.wait_ge(sSC, psa_half_sc[half])
                for j, kc in enumerate(kcs):
                    te.matmul(PSA[:, half * 1024 + j * 256: half * 1024 + (j + 1) * 256],
                              K_sb[0:64, ic * T + kc * 128: ic * T + (kc + 1) * 128],
                              rhs_e, start=True, stop=True, tile_position=(0, 0))
                    te.matmul(PSA[:, half * 1024 + 512 + j * 256: half * 1024 + 512 + (j + 1) * 256],
                              K_sb[64:128, ic * T + kc * 128: ic * T + (kc + 1) * 128],
                              rhs_o, start=True, stop=True, tile_position=(64, 0))
                te.mark(sTE, cTE)
                qk_mark[g] = cTE.v

            def emit_sv(g):
                kcs = list(range(g * 2, min(g * 2 + 2, nkc)))
                nk = len(kcs)
                assert nk == 2
                half = g % 2
                sc.wait_ge(sTE, qk_mark[g])
                if (g - 2) in pv_mark:
                    sc.wait_ge(sTE, pv_mark[g - 2])
                if runmax_ve[half]:
                    sc.wait_ge(sVE, runmax_ve[half])
                sc.activation(PB[:, half * 1024: half * 1024 + 1024],
                              PSA[:, half * 1024: half * 1024 + 1024],
                              ACT.Exp, scale=CST[:, 2:3]).then_inc(sSC, cSC.add(1))
                exp_mark[g] = cSC.v
                psa_half_sc[half] = cSC.v
                ve.wait_ge(sSC, exp_mark[g])
                if g == m:
                    ve.tensor_mul(PB[:, half * 1024: half * 1024 + 512],
                                  PB[:, half * 1024: half * 1024 + 512],
                                  TRI[:, 0:512])
                    ve.tensor_mul(PB[:, half * 1024 + 512: half * 1024 + 1024],
                                  PB[:, half * 1024 + 512: half * 1024 + 1024],
                                  TRI[:, 0:512])
                    ve.mark(sVE, cVE)
                ve.wait_ge(sVE, cVE.v)
                if g == 0:
                    if mhtrans_te:
                        ve.wait_ge(sTE, mhtrans_te)
                    ve.tensor_copy(MH[:, 0:1024],
                                   PB[:, half * 1024: half * 1024 + 1024])
                else:
                    ve.tensor_max(MH[:, 0:1024], MH[:, 0:1024],
                                  PB[:, half * 1024: half * 1024 + 1024])
                ve.mark(sVE, cVE)
                vm_mark[g] = cVE.v
                runmax_ve[half] = cVE.v

            def emit_pv(g):
                kcs = list(range(g * 2, min(g * 2 + 2, nkc)))
                half = g % 2
                te.wait_ge(sVE, vm_mark[g])
                if g == 0:
                    if num_free_sc[0]:
                        te.wait_ge(sSC, num_free_sc[0])
                    if num_free_sc[1]:
                        te.wait_ge(sSC, num_free_sc[1])
                for j, kc in enumerate(kcs):
                    te.matmul(PSN[0:65, 0:256],
                              V_sb[:, kc * (H * 65) + h_e * 65: kc * (H * 65) + h_e * 65 + 65],
                              PB[:, half * 1024 + j * 256: half * 1024 + (j + 1) * 256],
                              start=(kc == 0), stop=(kc == nkc - 1))
                    te.matmul(PSC[0:65, 0:256],
                              V_sb[:, kc * (H * 65) + h_o * 65: kc * (H * 65) + h_o * 65 + 65],
                              PB[:, half * 1024 + 512 + j * 256: half * 1024 + 512 + (j + 1) * 256],
                              start=(kc == 0), stop=(kc == nkc - 1))
                te.mark(sTE, cTE)
                pv_mark[g] = cTE.v
                return cTE.v

            emit_qk(0)
            if ngr > 1:
                emit_qk(1)
            emit_sv(0)
            if ngr > 2:
                emit_qk(2)
            if ngr > 1:
                emit_sv(1)
            pv_last = None
            for g in range(ngr):
                pv_last = emit_pv(g)
                if g + 3 < ngr:
                    emit_qk(g + 3)
                if g + 2 < ngr:
                    emit_sv(g + 2)
            sc.wait_ge(sTE, pv_last)
            sc.copy(NUMBUF[0:65, h_e * 256:(h_e + 1) * 256], PSN[0:65, 0:256])
            sc.copy(NUMBUF[0:65, h_o * 256:(h_o + 1) * 256],
                    PSC[0:65, 0:256]).then_inc(sSC, cSC.add(1))
            num_free_sc[0] = cSC.v
            num_free_sc[1] = cSC.v
            ve.wait_ge(sVE, cVE.v)
            ve.tensor_max(MH[:, 0:256], MH[:, 0:256], MH[:, 256:512])
            ve.tensor_max(MH[:, 512:768], MH[:, 512:768],
                          MH[:, 768:1024]).then_inc(sVE, cVE.add(1))
            pend = (hp, cVE.v)
        emit_pending()

        if m + 1 < NQB:
            te.wait_ge(sVE, cVE.v)
            for sub, src in ((0, LNTM[:, :]), (1, LNTMB[:, :])):
                te.wait_ge(sSC, cSC.v)
                for c in range(8):
                    te.transpose(PSB16[:, c * 128:(c + 1) * 128],
                                 src[:, c * 128:(c + 1) * 128], IDBF[:, :])
                te.mark(sTE, cTE)
                sc.wait_ge(sTE, cTE.v)
                dst = LNFM.ap().rearrange("p (dc t) -> p dc t", dc=8)[
                    :, :, sub * 128:(sub + 1) * 128]
                sc.copy(dst, PSB16.ap().rearrange("p (dc t) -> p dc t", dc=8)
                        ).then_inc(sSC, cSC.add(1))

        ve.wait_ge(sSC, tvexp_all_sc)
        ve.wait_ge(sVE, cVE.v)
        ve.tensor_max(TV[:, 0:32], TV[:, 0:32], CMAX[:, 0:32])
        ve.mark(sVE, cVE)
        te.wait_ge(sSC, cSC.v)
        te.wait_ge(sVE, cVE.v)
        for h in range(H):
            for qt in range(2):
                te.transpose(PSB16[:, 512 + (h * 2 + qt) * 2: 513 + (h * 2 + qt) * 2],
                             NUMBUF[64:65, h * 256 + qt * 128: h * 256 + (qt + 1) * 128],
                             IDBF[64:65, 64:65])
        te.mark(sTE, cTE)
        ve.wait_ge(sTE, cTE.v)
        den_ap = PSB16.ap().rearrange("p (c two) -> p c two", two=2)[:, 256:288, 0:1]
        tv_ap = TV.ap().rearrange("p (c one) -> p c one", one=1)[:, 0:32, 0:1]
        scr_ap = SCR.ap().rearrange("p (c one) -> p c one", one=1)[:, 48:80, 0:1]
        ve.scalar_tensor_tensor(scr_ap, tv_ap, EPS_ATTN,
                                den_ap, ALU.mult, ALU.add)
        ve.mark(sVE, cVE)
        ve.wait_ge(sVE, cVE.v)
        ve.reciprocal(SCR[:, 80:112], SCR[:, 48:80])
        ve.mark(sVE, cVE)
        recip_ve = cVE.v
        sc.wait_ge(sST, cST.v)
        sc.wait_ge(sVE, ln_ready_ve)
        ve.wait_ge(sST, cST.v)
        te.wait_ge(sVE, recip_ve)
        amult_ve = {}
        evac_sc = {}
        bc_mark = {}
        for rnd in range(2):
            if rnd == 1:
                te.wait_ge(sSC, cSC.v)
            for hl in range(8):
                h = rnd * 8 + hl
                for qt in range(2):
                    te.transpose(PSA[0:1, hl * 256 + qt * 128: hl * 256 + (qt + 1) * 128],
                                 SCR[:, 80 + h * 2 + qt:81 + h * 2 + qt], IDF[:, :])
            te.mark(sTE, cTE)
            rt_mark = cTE.v
            for hl in range(8):
                h = rnd * 8 + hl
                par = h % 2
                numps = PSN if par == 0 else PSC
                slot = (h // 2) % 2
                sslot = h % 2
                sc.wait_ge(sTE, rt_mark)
                if h >= 2:
                    sc.wait_ge(sTE, cTE.v)
                sc.copy(X2S[0:1, 1024 + sslot * 256: 1024 + (sslot + 1) * 256],
                        PSA[0:1, hl * 256:(hl + 1) * 256]).then_inc(sSC, cSC.add(1))
                evac_sc[h] = cSC.v
                te.wait_ge(sSC, evac_sc[h])
                if h >= 4 and (h - 4) in amult_ve:
                    te.wait_ge(sVE, amult_ve[h - 4])
                te.matmul(numps[0:64, slot * 256:(slot + 1) * 256],
                          ONESF[0:1, 0:64],
                          X2S[0:1, 1024 + sslot * 256: 1024 + (sslot + 1) * 256],
                          start=True, stop=True)
                te.mark(sTE, cTE)
                bc_mark[h] = cTE.v
                ve.wait_ge(sTE, cTE.v)
                if par == 0:
                    dst = ATTN[0:64, (h // 2) * 256:(h // 2 + 1) * 256]
                else:
                    dst = ATTS[0:64, (h // 2) * 256:(h // 2 + 1) * 256]
                ve.scalar_tensor_tensor(dst, NUMBUF[0:64, h * 256:(h + 1) * 256], 1.0,
                                        numps[0:64, slot * 256:(slot + 1) * 256],
                                        ALU.mult, ALU.mult).then_inc(sVE, cVE.add(1))
                amult_ve[h] = cVE.v
                if par == 1:
                    sync.wait_ge(sVE, amult_ve[h])
                    sync.dma_start(out=ATTN[64:128, (h // 2) * 256:(h // 2 + 1) * 256],
                                   in_=ATTS[0:64, (h // 2) * 256:(h // 2 + 1) * 256]
                                   ).then_inc(sSH, cSH.add(16))

        if m + 1 < NQB:
            te.wait_ge(sSC, cSC.v)
            te.wait_ge(sGD, cGD.v)
            pmarks = proj_1024(lambda ic: QF[:, ic * 256:(ic + 1) * 256])
        else:
            pmarks = [wb_free_te] * 8
        wo_src = wo_d.rearrange("(dc p) i -> p dc i", p=128)
        wb_dst = WB.ap().rearrange("p (dc i) -> p dc i", dc=8)
        for icw in range(8):
            gp.wait_ge(sTE, pmarks[icw])
            gp.dma_start(out=wb_dst[:, :, icw * 128:(icw + 1) * 128],
                         in_=wo_src[:, :, icw * 128:(icw + 1) * 128]
                         ).then_inc(sGD, cGD.add(16))

        te.wait_ge(sSH, cSH.v)
        te.wait_ge(sVE, cVE.v)
        te.wait_ge(sGD, cGD.v)
        te.wait_ge(sSC, cSC.v)
        for sub in range(2):
            for dh2 in range(2):
                for icc in range(8):
                    te.matmul(PSA[:, sub * 1024 + dh2 * 512: sub * 1024 + (dh2 + 1) * 512],
                              ATTN[:, icc * 256 + sub * 128: icc * 256 + (sub + 1) * 128],
                              WB[:, icc * D + dh2 * 512: icc * D + (dh2 + 1) * 512],
                              start=(icc == 0), stop=(icc == 7))
        te.mark(sTE, cTE)
        wb_free_te = cTE.v
        ve.wait_ge(sTE, cTE.v)
        ve.wait_ge(sLD, cLD.v)
        ve.wait_ge(sST, cST.v)
        ve.tensor_add(XTc[:, 0:D], XTc[:, 0:D], BOB[:, :])
        ve.tensor_add(XTc[:, D:2 * D], XTc[:, D:2 * D], BOB[:, :]).then_inc(sVE, cVE.add(1))
        ve.wait_ge(sVE, cVE.v)
        for sub in range(2):
            for dh2 in range(2):
                ve.scalar_tensor_tensor(
                    X2S[:, sub * 1024 + dh2 * 512: sub * 1024 + (dh2 + 1) * 512],
                    PSA[:, sub * 1024 + dh2 * 512: sub * 1024 + (dh2 + 1) * 512],
                    1.0, XTc[:, sub * D + dh2 * 512: sub * D + (dh2 + 1) * 512],
                    ALU.mult, ALU.add)
        ve.mark(sVE, cVE)
        xt_free_ve = cVE.v
        sync.wait_ge(sVE, cVE.v)
        sync.dma_start(out=x2_t[m], in_=x2s_ap).then_inc(sST, cST.add(16))
        if m + 2 < NQB:
            gp.wait_ge(sTE, cTE.v)
            gp.dma_start(out=WB.ap().rearrange("p (dc i) -> p dc i", dc=8),
                         in_=wq_d.rearrange("(dc p) i -> p dc i", p=128)
                         ).then_inc(sGD, cGD.add(16))

    nc.all_engine_barrier()

    wet_r = wet_d.rearrange("e (hc p) f -> p e hc f", p=128)
    wt_r = WT.ap().rearrange("p (e hc f) -> p e hc f", e=E, hc=8)
    for e in range(E):
        gp.dma_start(out=wt_r[:, e], in_=wet_r[:, e]).then_inc(sGD, cGD.add(16))
    a2_store = cST.v
    bsets = [(XT, LN2TM, LN2F32, MBC, sLD, cLD),
             (XTB2, LN2TMb, LN2F32, MBC, sLX, cLX)]
    hdr_ve = {}
    hdr_te = {}
    mask_ve = {}
    fin_ve = {}

    def hdrA(tt):
        XTb, TMb, F32b, MBCb_, sem, ct = bsets[tt % 2]
        if tt >= 2:
            sync.wait_ge(sVE, fin_ve.get(tt - 2, 0))
        sync.wait_ge(sST, a2_store)
        sync.dma_start(out=XTb.ap().rearrange("p (s c) -> p s c", s=2),
                       in_=x2_t[tt]).then_inc(sem, ct.add(16))
        ve.wait_ge(sem, ct.v)
        if tt >= 2:
            ve.wait_ge(sVE, cVE.v)
        ln_pair(TMb[:, 0:D], TMb[:, D:2 * D], dum=DUMB, srcb=XTb)
        hdr_ve[tt] = cVE.v

    def hdrB(tt):
        XTb, TMb, F32b, MBCb_, sem, ct = bsets[tt % 2]
        te.wait_ge(sVE, hdr_ve[tt])
        for sub in range(2):
            src = TMb[:, sub * D:(sub + 1) * D]
            for rnd in range(2):
                te.wait_ge(sSC, cSC.v)
                for c in range(4):
                    dc = rnd * 4 + c
                    te.transpose(PSB[:, c * 128:(c + 1) * 128],
                                 src[:, dc * 128:(dc + 1) * 128], IDF[:, :])
                te.mark(sTE, cTE)
                sc.wait_ge(sTE, cTE.v)
                dst = F32b.ap().rearrange("p (dc t) -> p dc t", dc=8)[
                    :, rnd * 4:(rnd + 1) * 4, sub * 128:(sub + 1) * 128]
                sc.copy(dst, PSB.ap().rearrange("p (c t) -> p c t", c=4)
                        ).then_inc(sSC, cSC.add(1))
        te.wait_ge(sSC, cSC.v)
        for sub in range(2):
            for hc in range(8):
                te.matmul(PSB[:, sub * 8:(sub + 1) * 8],
                          F32b[:, hc * 256 + sub * 128: hc * 256 + (sub + 1) * 128],
                          GWT[:, hc * 8:(hc + 1) * 8],
                          start=(hc == 0), stop=(hc == 7))
        te.mark(sTE, cTE)
        ve.wait_ge(sTE, cTE.v)
        if tt >= 1:
            ve.wait_ge(sVE, cVE.v)
        for sub in range(2):
            ve.reduce_max(MQ[:, sub:sub + 1], PSB[:, sub * 8:(sub + 1) * 8], axis=AX.X)
        ve.mark(sVE, cVE)
        ve.wait_ge(sVE, cVE.v)
        for sub in range(2):
            ve.tensor_scalar(EQ8[:, sub * 8:(sub + 1) * 8], PSB[:, sub * 8:(sub + 1) * 8],
                             MQ[:, sub:sub + 1], None, ALU.is_equal)
        ve.mark(sVE, cVE)
        ve.wait_ge(sVE, cVE.v)
        for sub in range(2):
            ve.scalar_tensor_tensor(SCR[:, 32 + sub * 8: 40 + sub * 8],
                                    EQ8[:, sub * 8:(sub + 1) * 8], -16.0,
                                    IOTA[:, 8:16], ALU.mult, ALU.add)
        ve.mark(sVE, cVE)
        ve.wait_ge(sVE, cVE.v)
        for sub in range(2):
            ve.tensor_reduce(IDX[:, sub:sub + 1], SCR[:, 32 + sub * 8: 40 + sub * 8],
                             AX.X, ALU.min)
        ve.mark(sVE, cVE)
        te.wait_ge(sVE, cVE.v)
        for sub in range(2):
            te.transpose(PSB[0:1, 16 + sub * 128: 16 + (sub + 1) * 128],
                         IDX[:, sub:sub + 1], IDF[:, :])
        te.mark(sTE, cTE)
        sc.wait_ge(sTE, cTE.v)
        sc.wait_ge(sST, cST.v)
        sc.copy(X2S[0:1, 1024:1280], PSB[0:1, 16:272]).then_inc(sSC, cSC.add(1))
        te.wait_ge(sSC, cSC.v)
        if tt >= 1:
            te.wait_ge(sVE, mask_ve.get(tt - 1, 0))
        te.matmul(PSC[:, 0:256], ONESF[0:1, 0:128], X2S[0:1, 1024:1280],
                  start=True, stop=True)
        te.mark(sTE, cTE)
        ve.wait_ge(sTE, cTE.v)
        ve.wait_ge(sVE, cVE.v)
        for e in range(E):
            ve.tensor_scalar(MBCb_[:, e * 256:(e + 1) * 256], PSC[:, 0:256],
                             float(e), None, ALU.is_equal)
        ve.mark(sVE, cVE)
        mask_ve[tt] = cVE.v
        hdr_te[tt] = cTE.v

    hdrA(0)
    hdrB(0)
    for tt in range(NQB):
        XTb, TMb, F32b, MBCb_, _, _ = bsets[tt % 2]
        if tt + 1 < NQB:
            hdrA(tt + 1)
        if tt == 0:
            te.wait_ge(sGD, cGD.v)
        NEH = E * 8
        mln_ve = {}
        mm_te = {}

        def emit_mln(i):
            e, hc = i // 8, i % 8
            buf = i % 4
            if (i - 4) in mm_te:
                ve.wait_ge(sTE, mm_te[i - 4])
            ve.tensor_mul(MLN[:, buf * 256:(buf + 1) * 256],
                          F32b[:, hc * 256:(hc + 1) * 256],
                          MBCb_[:, e * 256:(e + 1) * 256]).then_inc(sVE, cVE.add(1))
            mln_ve[i] = cVE.v

        ve.wait_ge(sVE, mask_ve[tt])
        ve.wait_ge(sTE, cTE.v)
        for ip in range(min(3, NEH)):
            emit_mln(ip)
        for i in range(NEH):
            if i + 3 < NEH:
                emit_mln(i + 3)
            buf = i % 4
            te.wait_ge(sVE, mln_ve[i])
            if i == 0:
                te.wait_ge(sVE, fin_ve.get(tt - 1, 0))
            for sub in range(2):
                for fh in range(2):
                    te.matmul(PSA[:, sub * 1024 + fh * 512: sub * 1024 + (fh + 1) * 512],
                              MLN[:, buf * 256 + sub * 128: buf * 256 + (sub + 1) * 128],
                              WT[:, i * D + fh * 512: i * D + (fh + 1) * 512],
                              start=(i == 0), stop=(i == NEH - 1))
            te.mark(sTE, cTE)
            mm_te[i] = cTE.v
        ve.wait_ge(sTE, cTE.v)
        ve.wait_ge(sST, cST.v)
        for sub in range(2):
            for fh in range(2):
                ve.scalar_tensor_tensor(
                    X2S[:, sub * 1024 + fh * 512: sub * 1024 + (fh + 1) * 512],
                    PSA[:, sub * 1024 + fh * 512: sub * 1024 + (fh + 1) * 512],
                    1.0, TMb[:, sub * D + fh * 512: sub * D + (fh + 1) * 512],
                    ALU.mult, ALU.add)
        ve.mark(sVE, cVE)
        ve.wait_ge(sVE, cVE.v)
        for sub in range(2):
            for fh in range(2):
                ve.tensor_add(
                    X2S[:, sub * 1024 + fh * 512: sub * 1024 + (fh + 1) * 512],
                    X2S[:, sub * 1024 + fh * 512: sub * 1024 + (fh + 1) * 512],
                    XTb[:, sub * D + fh * 512: sub * D + (fh + 1) * 512])
        ve.mark(sVE, cVE)
        fin_ve[tt] = cVE.v
        sync.wait_ge(sVE, cVE.v)
        sync.dma_start(out=out_t[tt], in_=x2s_ap).then_inc(sST, cST.add(16))
        if tt + 1 < NQB:
            hdrB(tt + 1)

    sync.wait_ge(sST, cST.v)
    if dbg is not None:
        dbg.update(locals())
    return nc


def _consts():
    ident_bf = np.eye(128, dtype=ml_dtypes.bfloat16)
    ident_f = np.eye(128, dtype=np.float32)
    k = np.arange(128)[:, None]
    q = np.arange(128)[None, :]
    tri = (k <= q).astype(np.float32)
    tri0 = np.concatenate([tri, np.ones((128, 128), np.float32)], axis=1)
    tri1 = np.concatenate([np.zeros((128, 128), np.float32), tri], axis=1)
    tri_all = np.concatenate([tri0, tri1], axis=1).astype(ml_dtypes.bfloat16)
    iota = np.broadcast_to(np.arange(8, dtype=np.float32), (128, 8))
    iota16 = np.concatenate([iota, iota + 16.0], axis=1).astype(np.float32)
    ones_f = np.ones((1, 128), dtype=np.float32)
    cst = np.zeros((128, 8), dtype=np.float32)
    cst[:, 0] = EPS_LN
    cst[:, 1] = -0.5
    cst[:, 2] = SCALE
    return ident_bf, ident_f, tri_all, np.ascontiguousarray(iota16), ones_f, cst


def kernel(L0_attn_x, L0_attn_Wq, L0_attn_Wkv, L0_attn_Wo, L0_attn_bo,
           L0_ffn_gate_w, L0_ffn_expert_w):
    x = np.ascontiguousarray(np.asarray(L0_attn_x, dtype=np.float32))
    wq = np.ascontiguousarray(np.asarray(L0_attn_Wq, dtype=np.float32))
    wkv = np.ascontiguousarray(np.asarray(L0_attn_Wkv, dtype=np.float32))
    wo = np.ascontiguousarray(np.asarray(L0_attn_Wo, dtype=np.float32))
    bo = np.asarray(L0_attn_bo, dtype=np.float32)
    gw = np.asarray(L0_ffn_gate_w, dtype=np.float32)
    we = np.asarray(L0_ffn_expert_w, dtype=np.float32)
    B, T, _ = x.shape
    bo_b = np.ascontiguousarray(np.broadcast_to(bo[None, :], (128, D)))
    gwt = np.ascontiguousarray(gw.T)
    wet = np.ascontiguousarray(we.transpose(0, 2, 1))
    ident_bf, ident_f, tri_all, iota16, ones_f, cst = _consts()

    nc = build(T=T)
    shared = {
        "wq": wq, "wkv": wkv, "wo": wo, "bo_b": bo_b, "gwt": gwt, "wet": wet,
        "id_bf": ident_bf, "id_f32": ident_f, "tri": tri_all,
        "iota8": iota16, "ones_f": ones_f, "cst": cst,
    }
    in_maps = [dict(shared, x=np.ascontiguousarray(x[b])) for b in range(B)]
    res = run_bass_kernel_spmd(nc, in_maps, core_ids=list(range(B)))
    out = np.stack([np.asarray(r["out"]) for r in res.results], axis=0)
    return out.astype(np.float32)
